# version 07 (final)
# baseline (speedup 1.0000x reference)
import contextlib
import math
import os
DEBUG = bool(int(os.environ.get("KDEBUG", "0")))

import numpy as np
import ml_dtypes
import concourse.bass as bass
import concourse.mybir as mybir
from concourse.bass_utils import run_bass_kernel_spmd

F32 = mybir.dt.float32
BF16 = mybir.dt.bfloat16
AF = mybir.ActivationFunctionType
ALU = mybir.AluOpType
AX = mybir.AxisListType
BFnp = ml_dtypes.bfloat16

NCORES = 8
TC = 2048
G = 16
E = 4096
HK = 8
NKC = 32
NS2 = 16
CAP = float(int(3.0 * 16384 / 8))
SM_SCALE = 1.0 / math.sqrt(128.0)
CSHIFT = 24.0
LN_EPS = 1e-5
ALPHA = 1.0 / 16.0


def build_nc():
    nc = bass.Bass("TRN2", target_bir_lowering=False)

    d_x = nc.dram_tensor("x", [TC, E], F32, kind="ExternalInput")
    d_xT = nc.dram_tensor("xT", [E, TC], F32, kind="ExternalInput")
    d_wqr = nc.dram_tensor("wqr", [E, 4, HK, 128], F32, kind="ExternalInput")
    d_wkr = nc.dram_tensor("wkr", [E, HK, 128], F32, kind="ExternalInput")
    d_wvr = nc.dram_tensor("wvr", [E, HK, 128], F32, kind="ExternalInput")
    d_bqk4 = nc.dram_tensor("bqk4r", [36, 1024], F32, kind="ExternalInput")
    d_ones4 = nc.dram_tensor("ones4r", [36, 128], F32, kind="ExternalInput")
    d_bvr = nc.dram_tensor("bvr", [128, HK], F32, kind="ExternalInput")
    d_wmT = nc.dram_tensor("wmT", [E, E], F32, kind="ExternalInput")
    d_astk = nc.dram_tensor("astk", [E, 128], F32, kind="ExternalInput")
    d_bstk = nc.dram_tensor("bstk", [128, E], F32, kind="ExternalInput")
    d_gw1T = nc.dram_tensor("gw1T", [E, 32], F32, kind="ExternalInput")
    d_gw2T = nc.dram_tensor("gw2T", [32, 8], F32, kind="ExternalInput")
    d_gb1 = nc.dram_tensor("gb1r", [1, 32], F32, kind="ExternalInput")
    d_lng = nc.dram_tensor("lngr", [1, 32], F32, kind="ExternalInput")
    d_lnb = nc.dram_tensor("lnbr", [1, 32], F32, kind="ExternalInput")
    d_gb2 = nc.dram_tensor("gb2r", [1, 8], F32, kind="ExternalInput")
    d_idb = nc.dram_tensor("idb", [128, 128], BF16, kind="ExternalInput")
    d_idf = nc.dram_tensor("idf", [128, 128], F32, kind="ExternalInput")
    d_ones = nc.dram_tensor("onesr", [1, 128], F32, kind="ExternalInput")
    d_onesc = nc.dram_tensor("onesc", [128, 1], F32, kind="ExternalInput")
    d_selc = nc.dram_tensor("selc", [128, 8], F32, kind="ExternalInput")

    d_out = nc.dram_tensor("out", [TC, E], F32, kind="ExternalOutput")
    if DEBUG:
        d_dbgqk = nc.dram_tensor("dbgqk", [TC, 2048], BF16, kind="ExternalOutput")
        d_dbgv = nc.dram_tensor("dbgv", [128, HK, TC], BF16, kind="ExternalOutput")
        d_dbgy = nc.dram_tensor("dbgy", [TC, E], BF16, kind="ExternalOutput")
        d_dbgz = nc.dram_tensor("dbgz", [TC, E], BF16, kind="ExternalOutput")
        d_dbgg = nc.dram_tensor("dbgg", [128, 128], F32, kind="ExternalOutput")
        d_dbgw = nc.dram_tensor("dbgw", [128, 128], F32, kind="ExternalOutput")
        d_dbgc = nc.dram_tensor("dbgc", [1, 48], F32, kind="ExternalOutput")
        d_dbgo = nc.dram_tensor("dbgo", [128, 1024], F32, kind="ExternalOutput")
        d_dbgr = nc.dram_tensor("dbgr", [128, 128], F32, kind="ExternalOutput")
        d_dbgl = nc.dram_tensor("dbgl", [128, 8], F32, kind="ExternalOutput")
        d_dbgu = nc.dram_tensor("dbgu", [128, 2048], BF16, kind="ExternalOutput")
        d_dbgwq = nc.dram_tensor("dbgwq", [NKC, 128, 2048], BF16, kind="ExternalOutput")

    d_wqgk = nc.dram_tensor("s_wqgk", [NKC, 128, 2048], BF16)
    d_qk = nc.dram_tensor("s_qk", [TC, 2048], BF16)
    d_vfm = nc.dram_tensor("s_vfm", [128, HK, TC], BF16)
    d_y = nc.dram_tensor("s_y", [TC, E], BF16)
    d_z2 = nc.dram_tensor("s_z2", [TC, E], BF16)
    d_ccin = nc.dram_tensor("s_ccin", [128, 128], F32)
    d_ccout = nc.dram_tensor("s_ccout", [128, 128], F32)

    with contextlib.ExitStack() as SO:
        def alloc(stack, name, shape, dt):
            return stack.enter_context(nc.sbuf_tensor(name, shape, dt))

        idb = alloc(SO, "idb_sb", [128, 128], BF16)
        idf = alloc(SO, "idf_sb", [128, 128], F32)
        onesr = alloc(SO, "ones_sb", [1, 128], F32)
        onesc = alloc(SO, "onesc_sb", [128, 1], F32)
        selc = alloc(SO, "selc_sb", [128, 8], F32)
        cvec = alloc(SO, "cvec", [128, 1], F32)
        zvec = alloc(SO, "zvec", [128, 1], F32)
        evec = alloc(SO, "evec", [128, 1], F32)
        bv_sb = alloc(SO, "bv_sb", [128, HK], F32)
        gb1r = alloc(SO, "gb1r_sb", [1, 32], F32)
        lngr = alloc(SO, "lngr_sb", [1, 32], F32)
        lnbr = alloc(SO, "lnbr_sb", [1, 32], F32)
        gb2r = alloc(SO, "gb2r_sb", [1, 8], F32)
        gw2t = alloc(SO, "gw2t_sb", [32, 8], F32)
        psum = SO.enter_context(nc.psum_tensor("ps", [128, 4096], F32))

        with contextlib.ExitStack() as S0:
            xT = alloc(S0, "xT_sb", [128, NKC * 2048], BF16)
            bqk4 = alloc(S0, "bqk4_sb", [36, 1024], F32)
            ones4 = alloc(S0, "ones4_sb", [36, 128], F32)
            qkbias = alloc(S0, "qkbias", [128, 2048], BF16)
            wv_sb = [alloc(S0, f"wv_sb{i}", [128, 4096], BF16) for i in range(2)]
            vtmp = alloc(S0, "vtmp", [128, 2048], BF16)
            reg = alloc(S0, "reg_sb", [128, 24576], BF16)
            wq_ld = [reg[:, i * 4096:(i + 1) * 4096] for i in range(4)]
            wst = [reg[:, 16384 + i * 2048:16384 + (i + 1) * 2048]
                   for i in range(4)]
            wsl = [reg[:, 0:4096], reg[:, 4096:8192]]
            qkev = [reg[:, 8192:9216], reg[:, 9216:10240]]

            SEMS = contextlib.ExitStack()
            s_c = SEMS.enter_context(nc.semaphore("a_c"))
            s_xt = SEMS.enter_context(nc.semaphore("a_xt"))
            s_wl = [SEMS.enter_context(nc.semaphore(f"a_wl{i}"))
                    for i in range(4)]
            s_wk = [SEMS.enter_context(nc.semaphore(f"a_wk{i}"))
                    for i in range(4)]
            s_fd = SEMS.enter_context(nc.semaphore("a_fd"))
            s_st = [SEMS.enter_context(nc.semaphore(f"a_st{i}"))
                    for i in range(4)]
            s_hp = SEMS.enter_context(nc.semaphore("a_hp"))
            s_bq2 = SEMS.enter_context(nc.semaphore("a_bq"))
            s_pe0 = SEMS.enter_context(nc.semaphore("a_pe0"))
            s_wv0 = SEMS.enter_context(nc.semaphore("a_wv0"))
            s_wv1 = SEMS.enter_context(nc.semaphore("a_wv1"))
            s_vm = SEMS.enter_context(nc.semaphore("a_vm"))
            s_vt = SEMS.enter_context(nc.semaphore("a_vt"))
            s_vs = SEMS.enter_context(nc.semaphore("a_vs"))
            s_qw0 = SEMS.enter_context(nc.semaphore("a_qw0"))
            s_qw1 = SEMS.enter_context(nc.semaphore("a_qw1"))
            s_qn = SEMS.enter_context(nc.semaphore("a_qn"))
            s_qe = SEMS.enter_context(nc.semaphore("a_qe"))
            s_qs0 = SEMS.enter_context(nc.semaphore("a_qs0"))
            s_qs1 = SEMS.enter_context(nc.semaphore("a_qs1"))
            with SEMS, nc.Block("p01a") as blk:

                @blk.sync
                def _(sy):
                    consts = [
                        (idb, d_idb[:, :]), (idf, d_idf[:, :]),
                        (onesr, d_ones[:, :]), (onesc, d_onesc[:, :]),
                        (selc, d_selc[:, :]),
                        (bv_sb, d_bvr[:, :]), (gb1r, d_gb1[:, :]),
                        (lngr, d_lng[:, :]), (lnbr, d_lnb[:, :]),
                        (gb2r, d_gb2[:, :]), (gw2t, d_gw2T[:, :]),
                        (bqk4, d_bqk4[:, :]), (ones4, d_ones4[:, :]),
                    ]
                    for dst, src in consts:
                        sy.dma_start(dst[:, :], src).then_inc(s_c, 16)
                    for i in range(4):
                        sy.wait_ge(s_st[i], 16 * 8)
                    for qh in range(8):
                        h = qh % 2
                        for kq in range(8):
                            n = qh * 8 + kq
                            if n >= 2:
                                sy.wait_ge(s_qn, n - 1)
                            sy.dma_start(
                                wsl[n % 2].rearrange("p (k c) -> p k c", c=1024),
                                d_wqgk[4 * kq:4 * kq + 4, :,
                                       h * 1024:(h + 1) * 1024].rearrange(
                                    "k p c -> p k c"),
                            ).then_inc([s_qw0, s_qw1][n % 2], 16)

                @blk.gpsimd
                def _(gp):
                    gp.memset(cvec[:, :], -CSHIFT)
                    gp.memset(zvec[:, :], 0.0)
                    gp.memset(evec[:, :], LN_EPS)
                    src = d_xT[:, :].rearrange("(k p) t -> p k t", p=128)
                    for q in range(4):
                        gp.dma_start(
                            xT[:, q * 8 * 2048:(q + 1) * 8 * 2048].rearrange(
                                "p (k t) -> p k t", t=2048),
                            src[:, q * 8:(q + 1) * 8, :],
                        ).then_inc(s_xt, 16)
                    for hk in range(2):
                        gp.dma_start(
                            wv_sb[hk][:, :].rearrange("p (k d) -> p k d", d=128),
                            d_wvr[:, hk, :].rearrange("(k p) d -> p k d", p=128),
                        ).then_inc([s_wv0, s_wv1][hk], 16)
                    wsched = {4: 2, 10: 3, 14: 4, 18: 5, 22: 6, 26: 7}
                    for k in range(NKC):
                        if k >= 4:
                            gp.wait_ge(s_st[k % 4], 16 * (k // 4))
                        gp.dma_start(
                            wq_ld[k % 4],
                            d_wqr[:, :, :, :].rearrange(
                                "(k p) a b c -> k p (a b c)", p=128)[k],
                        ).then_inc(s_wl[k % 4], 16)
                        gp.dma_start(
                            wst[k % 4][:, 1024:2048],
                            d_wkr[:, :, :].rearrange(
                                "(k p) a b -> k p (a b)", p=128)[k],
                        ).then_inc(s_wk[k % 4], 16)
                        if k in wsched:
                            hk = wsched[k]
                            gp.wait_ge(s_vm, hk - 1)
                            gp.dma_start(
                                wv_sb[hk % 2][:, :].rearrange(
                                    "p (k d) -> p k d", d=128),
                                d_wvr[:, hk, :].rearrange(
                                    "(k p) d -> p k d", p=128),
                            ).then_inc([s_wv0, s_wv1][hk % 2], 16)

                @blk.tensor
                def _(te):
                    te.wait_ge(s_c, 16 * 13)
                    for half in range(2):
                        for w in range(2):
                            mm = te.matmul(
                                psum[:, half * 1024 + w * 512:
                                     half * 1024 + (w + 1) * 512],
                                ones4[32 * half:32 * half + 4, :],
                                bqk4[32 * half:32 * half + 4,
                                     w * 512:(w + 1) * 512],
                                start=True, stop=True,
                                skip_group_check=True)
                            mm.then_inc(s_pe0, 1)
                    te.wait_ge(s_xt, 64)
                    te.wait_ge(s_bq2, 1)
                    for hk in range(HK):
                        te.wait_ge([s_wv0, s_wv1][hk % 2],
                                   16 * (hk // 2 + 1))
                        if hk >= 2:
                            te.wait_ge(s_vt, hk - 1)
                        base = 0 if hk % 2 == 0 else 4
                        for k in range(NKC):
                            for w in range(4):
                                mm = te.matmul(
                                    psum[:, (base + w) * 512:(base + w + 1) * 512],
                                    wv_sb[hk % 2][:, k * 128:(k + 1) * 128],
                                    xT[:, k * 2048 + w * 512:
                                       k * 2048 + (w + 1) * 512],
                                    start=(k == 0), stop=(k == NKC - 1),
                                    skip_group_check=True)
                        mm.then_inc(s_vm, 1)
                    te.wait_ge(s_vt, 8)
                    for qh in range(8):
                        q, h = qh // 2, qh % 2
                        if qh >= 1:
                            te.wait_ge(s_qe, 4 * qh)
                        for kq in range(8):
                            nn = qh * 8 + kq
                            te.wait_ge([s_qw0, s_qw1][nn % 2],
                                       16 * (nn // 2 + 1))
                            for kk in range(4):
                                k = 4 * kq + kk
                                for sg in range(4):
                                    tok0 = (q * 4 + sg) * 128
                                    for i in range(2):
                                        mm = te.matmul(
                                            psum[:, sg * 1024 + i * 512:
                                                 sg * 1024 + (i + 1) * 512],
                                            xT[:, k * 2048 + tok0:
                                               k * 2048 + tok0 + 128],
                                            wsl[(qh * 8 + kq) % 2][
                                                :, kk * 1024 + i * 512:
                                                kk * 1024 + (i + 1) * 512],
                                            start=(k == 0), stop=(k == NKC - 1),
                                            skip_group_check=True)
                            mm.then_inc(s_qn, 1)

                @blk.vector
                def _(v):
                    hv = [0]

                    def hop(instr):
                        instr.then_inc(s_hp, 1)
                        hv[0] += 1
                        v.wait_ge(s_hp, hv[0])

                    def fold(k):
                        v.wait_ge(s_wl[k % 4], 16 * (k // 4 + 1))
                        if k >= 4:
                            v.wait_ge(s_st[k % 4], 16 * (k // 4))
                        b = wq_ld[k % 4]
                        w = wst[k % 4][:, 0:1024]
                        hop(v.tensor_tensor(w, b[:, 0:1024],
                                            b[:, 1024:2048], ALU.add))
                        hop(v.tensor_tensor(w, w, b[:, 2048:3072], ALU.add))
                        v.tensor_tensor(w, w, b[:, 3072:4096],
                                        ALU.add).then_inc(s_fd, 1)

                    def vevac(hk):
                        v.wait_ge(s_vm, hk + 1)
                        if hk >= 1:
                            v.wait_ge(s_vs, 16 * hk)
                        base = 0 if hk % 2 == 0 else 4
                        v.tensor_scalar(vtmp[:, :],
                                        psum[:, base * 512:(base + 4) * 512],
                                        bv_sb[:, hk:hk + 1], None,
                                        ALU.add).then_inc(s_vt, 1)

                    sched = {2: -1, 8: 0, 11: 1, 14: 2, 18: 3, 22: 4,
                             26: 5, 29: 6}
                    for k in range(NKC):
                        fold(k)
                        if k in sched:
                            if sched[k] == -1:
                                v.wait_ge(s_pe0, 4)
                                for w in range(4):
                                    cp = v.tensor_copy(
                                        qkbias[:, w * 512:(w + 1) * 512],
                                        psum[:, w * 512:(w + 1) * 512])
                                cp.then_inc(s_bq2, 1)
                            else:
                                vevac(sched[k])
                    vevac(7)
                    for qh in range(8):
                        v.wait_ge(s_qn, 8 * (qh + 1))
                        for sg in range(4):
                            n = qh * 4 + sg
                            if n >= 2:
                                v.wait_ge([s_qs0, s_qs1][n % 2], 16 * (n // 2))
                            v.tensor_tensor(
                                qkev[n % 2],
                                psum[:, sg * 1024:(sg + 1) * 1024],
                                qkbias[:, (qh % 2) * 1024:(qh % 2 + 1) * 1024],
                                ALU.add).then_inc(s_qe, 1)

                @blk.scalar
                def _(sc):
                    vsched = {9: 0, 12: 1, 16: 2, 20: 3, 24: 4, 27: 5, 30: 6}
                    for k in range(NKC):
                        sc.wait_ge(s_fd, k + 1)
                        sc.wait_ge(s_wk[k % 4], 16 * (k // 4 + 1))
                        sc.dma_start(d_wqgk[k], wst[k % 4][:, :]).then_inc(
                            s_st[k % 4], 16)
                        if k in vsched:
                            hk = vsched[k]
                            sc.wait_ge(s_vt, hk + 1)
                            sc.dma_start(d_vfm[:, hk, :],
                                         vtmp[:, :]).then_inc(s_vs, 16)
                    sc.wait_ge(s_vt, 8)
                    sc.dma_start(d_vfm[:, 7, :],
                                 vtmp[:, :]).then_inc(s_vs, 16)
                    sc.wait_ge(s_vs, 16 * 8)
                    for qh in range(8):
                        q, h = qh // 2, qh % 2
                        for sg in range(4):
                            n = qh * 4 + sg
                            sc.wait_ge(s_qe, n + 1)
                            r0 = (q * 4 + sg) * 128
                            sc.dma_start(
                                d_qk[r0:r0 + 128, h * 1024:(h + 1) * 1024],
                                qkev[n % 2]).then_inc(
                                [s_qs0, s_qs1][n % 2], 16)
                    sc.wait_ge(s_qs0, 16 * 16)
                    sc.wait_ge(s_qs1, 16 * 16)

        with contextlib.ExitStack() as S1:
            yT = alloc(S1, "yT", [128, NKC * 2048], BF16)
            uwT = alloc(S1, "uwT", [128, 2048], BF16)

            with contextlib.ExitStack() as SB:
                qg_pack = [alloc(SB, f"qgp{i}", [128, 4096], BF16) for i in range(2)]
                k_pack = [alloc(SB, f"kp{i}", [128, 4096], BF16) for i in range(2)]
                vbuf = alloc(SB, "vbuf", [128, 1152], BF16)
                pprime = alloc(SB, "pprime", [128, 4096], BF16)
                o_all = alloc(SB, "o_all", [128, 1024], F32)
                xg = alloc(SB, "xg", [128, 4096], F32)
                ytok = alloc(SB, "ytok", [128, 4096], BF16)
                rall = alloc(SB, "rall", [128, 128], F32)
                rtr = rall
                lnt = alloc(SB, "lnt", [128, 8], F32)

                SEMS = contextlib.ExitStack()
                s_pk = [SEMS.enter_context(nc.semaphore("b_pk0")),
                        SEMS.enter_context(nc.semaphore("b_pk1"))]
                s_vb = SEMS.enter_context(nc.semaphore("b_vb"))
                s_mb = SEMS.enter_context(nc.semaphore("b_mb"))
                s_xg = SEMS.enter_context(nc.semaphore("b_xg"))
                s_sc = SEMS.enter_context(nc.semaphore("b_sc"))
                s_ex = SEMS.enter_context(nc.semaphore("b_ex"))
                s_av = SEMS.enter_context(nc.semaphore("b_av"))
                s_rc = SEMS.enter_context(nc.semaphore("b_rc"))
                s_oe = SEMS.enter_context(nc.semaphore("b_oe"))
                s_ot = SEMS.enter_context(nc.semaphore("b_ot"))
                s_on = SEMS.enter_context(nc.semaphore("b_on"))
                s_z1 = SEMS.enter_context(nc.semaphore("b_z1"))
                s_ldv = SEMS.enter_context(nc.semaphore("b_ldv"))
                s_lac = SEMS.enter_context(nc.semaphore("b_lac"))
                s_y = SEMS.enter_context(nc.semaphore("b_y"))
                s_ys = SEMS.enter_context(nc.semaphore("b_ys"))
                s_yt = SEMS.enter_context(nc.semaphore("b_yt"))
                s_ye = SEMS.enter_context(nc.semaphore("b_ye"))
                s_hv = SEMS.enter_context(nc.semaphore("b_hv"))
                s_ha = SEMS.enter_context(nc.semaphore("b_ha"))
                with SEMS, nc.Block("p1b") as blk:

                    @blk.gpsimd
                    def _(gp):
                        gp.memset(vbuf[:, 1024:1152], 1.0).then_inc(s_mb, 1)
                        for g in range(G):
                            if g >= 2:
                                gp.wait_ge(s_sc, (g - 1) * 8)
                            for half, pk in ((0, qg_pack), (1, k_pack)):
                                src = d_qk[g * 128:(g + 1) * 128,
                                           half * 1024:(half + 1) * 1024].rearrange(
                                    "(p j) (hk d) -> j hk p d", j=4, hk=HK)
                                for j in range(4):
                                    gp.dma_start(
                                        pk[g % 2][32 * j:32 * j + HK, :].rearrange(
                                            "hk (p d) -> hk p d", p=32),
                                        src[j],
                                    ).then_inc(s_pk[g % 2], 16)

                    @blk.sync
                    def _(sy):
                        for g in range(G):
                            if g >= 1:
                                sy.wait_ge(s_av, g * 8)
                            sy.dma_start(
                                vbuf[:, 0:1024].rearrange(
                                    "p (hk t) -> p hk t", hk=HK),
                                d_vfm[:, :, g * 128:(g + 1) * 128],
                            ).then_inc(s_vb, 16)
                            if g >= 1:
                                sy.wait_ge(s_y, g)
                            sy.dma_start(xg[:, :],
                                         d_x[g * 128:(g + 1) * 128, :]
                                         ).then_inc(s_xg, 16)
                            sy.wait_ge(s_y, g + 1)
                            sy.dma_start(d_y[g * 128:(g + 1) * 128, :],
                                         ytok[:, :]).then_inc(s_ys, 16)
                        sy.wait_ge(s_ys, 16 * G)
                        if DEBUG:
                            sy.wait_ge(s_ot, G)
                            sy.dma_start(d_dbgo[:, :], o_all[:, :]).then_inc(s_ys, 16)
                            sy.dma_start(d_dbgr[:, :], rall[:, :]).then_inc(s_ys, 16)
                            sy.wait_ge(s_y, G)
                            sy.dma_start(d_dbgl[:, :], lnt[:, :]).then_inc(s_ys, 16)
                            sy.wait_ge(s_ys, 16 * (G + 3))

                    @blk.tensor
                    def _(te):
                        for g in range(G):
                            fi0 = g * 8
                            te.wait_ge(s_pk[g % 2], 16 * 8 * (g // 2 + 1))
                            te.wait_ge(s_vb, 16 * (g + 1))
                            te.wait_ge(s_mb, 1)
                            if g >= 1:
                                te.wait_ge(s_ye, 8 * g)

                            def scores(f):
                                fi = fi0 + f
                                if fi >= 2:
                                    te.wait_ge(s_oe, fi - 1)
                                    te.wait_ge(s_rc, fi - 1)
                                sp = (f % 2) * 2048
                                for i in range(16):
                                    t = 16 * f + i
                                    p, j = t // 4, t % 4
                                    w = i // 4
                                    mm = te.matmul(
                                        psum[:, sp + j * 512 + w * 128:
                                             sp + j * 512 + (w + 1) * 128],
                                        k_pack[g % 2][32 * j:32 * j + 8,
                                                      p * 128:(p + 1) * 128],
                                        qg_pack[g % 2][32 * j:32 * j + 8,
                                                       p * 128:(p + 1) * 128],
                                        start=(w == 0), stop=(w == 3),
                                        tile_position=(32 * j, 0),
                                        skip_group_check=True)
                                mm.then_inc(s_sc, 1)

                            def av(f):
                                fi = fi0 + f
                                sp = (f % 2) * 2048
                                order = [i for i in range(16) if i % 4 < 2] \
                                    + [i for i in range(16) if i % 4 >= 2]
                                te.wait_ge(s_ex, 2 * fi + 1)
                                for n, i in enumerate(order):
                                    if n == 8:
                                        te.wait_ge(s_ex, 2 * fi + 2)
                                    t = 16 * f + i
                                    p, j = t // 4, t % 4
                                    w = i // 4
                                    mm = te.matmul(
                                        psum[:, sp + i * 12: sp + i * 12 + 9],
                                        pprime[:, sp + j * 512 + w * 128:
                                               sp + j * 512 + (w + 1) * 128],
                                        vbuf[:, t::128],
                                        start=(n == 0), stop=(n == 15),
                                        skip_group_check=True)
                                mm.then_inc(s_av, 1)

                            scores(0)
                            scores(1)
                            for f in range(8):
                                av(f)
                                if f + 2 < 8:
                                    scores(f + 2)
                            te.wait_ge(s_oe, fi0 + 8)
                            te.wait_ge(s_rc, fi0 + 8)
                            if g >= 1:
                                te.wait_ge(s_z1, g)
                            for hk in range(HK):
                                b, w = hk // 2, hk % 2
                                mm = te.matmul(
                                    psum[:, b * 512 + w * 128:
                                         b * 512 + (w + 1) * 128],
                                    o_all[:, hk * 128:(hk + 1) * 128],
                                    idf[:, :], is_transpose=True,
                                    start=(w == 0),
                                    stop=(w == 1 and b > 0),
                                    skip_group_check=True)
                            te.matmul(psum[:, 256:384], rall[:, :], idf[:, :],
                                      is_transpose=True, start=False, stop=True,
                                      skip_group_check=True).then_inc(s_ot, 1)
                            te.wait_ge(s_y, g + 1)
                            te.wait_ge(s_z1, g + 1)
                            for b8 in range(8):
                                bc = psum[:, b8 * 512:(b8 + 1) * 512].bitcast(BF16)
                                for w in range(4):
                                    c = b8 * 4 + w
                                    mm = te.matmul(
                                        bc[:, w * 128:(w + 1) * 128],
                                        ytok[:, c * 128:(c + 1) * 128],
                                        idb[:, :], is_transpose=True,
                                        start=(w == 0), stop=(w == 3),
                                        skip_group_check=True)
                                mm.then_inc(s_yt, 1)

                    @blk.scalar
                    def _(sc):
                        for g in range(G):
                            fi0 = g * 8
                            for f in range(8):
                                fi = fi0 + f
                                sc.wait_ge(s_sc, fi + 1)
                                if fi >= 2:
                                    sc.wait_ge(s_av, fi - 1)
                                sp = (f % 2) * 2048
                                for hh in range(2):
                                    sc.activation(
                                        pprime[:, sp + hh * 1024:
                                               sp + (hh + 1) * 1024],
                                        psum[:, sp + hh * 1024:
                                             sp + (hh + 1) * 1024], AF.Exp,
                                        bias=cvec[:, :],
                                        scale=SM_SCALE).then_inc(s_ex, 1)
                            sc.wait_ge(s_z1, g + 1)
                            if g >= 1:
                                sc.wait_ge(s_ys, 16 * g)
                                sc.wait_ge(s_yt, 8 * g)
                            sc.activation(ytok[:, :], xg[:, :], AF.Square, bias=zvec[:, :], scale=1.0,
                                          accum_out=lnt[:, 1:2]).then_inc(s_lac, 1)
                            sc.wait_ge(s_ldv, 3 * g + 2)
                            sc.activation(lnt[:, 4:5], lnt[:, 4:5], AF.Ln,
                                          bias=evec[:, :],
                                          scale=1.0).then_inc(s_ha, 1)
                            sc.wait_ge(s_ha, g + 1)
                            sc.activation(lnt[:, 5:6], lnt[:, 4:5], AF.Exp,
                                          bias=zvec[:, :], scale=-0.5).then_inc(s_lac, 1)
                            sc.wait_ge(s_ldv, 3 * g + 3)
                            sc.activation(ytok[:, :], xg[:, :], AF.Identity,
                                          bias=lnt[:, 7:8],
                                          scale=lnt[:, 5:6]).then_inc(s_y, 1)

                    @blk.vector
                    def _(v):
                        for g in range(G):
                            fi0 = g * 8
                            for f in range(8):
                                fi = fi0 + f
                                v.wait_ge(s_av, fi + 1)
                                if f == 0 and g >= 1:
                                    v.wait_ge(s_ot, g)
                                sp = (f % 2) * 2048
                                v.reciprocal(rall[:, 16 * f:16 * f + 16],
                                             psum[:, sp + 8: sp + 192:12]
                                             ).then_inc(s_rc, 1)
                                v.tensor_copy(
                                    o_all[:, :].rearrange(
                                        "p (hk t) -> p t hk", hk=HK)
                                    [:, 16 * f:16 * f + 16, :],
                                    psum[:, sp:sp + 192].rearrange(
                                        "p (t c) -> p t c", c=12)[:, :, 0:8],
                                ).then_inc(s_oe, 1)
                            v.wait_ge(s_ot, g + 1)
                            v.tensor_copy(rtr[:, :],
                                          psum[:, 256:384]).then_inc(s_on, 1)
                            v.wait_ge(s_on, 2 * g + 1)
                            otr = psum[:, 0:2048].rearrange(
                                "p (b c) -> p b c", c=512)[:, :, 0:256].rearrange(
                                "p b (w d) -> p b w d", d=128)
                            onout = psum[:, 2048:3072].rearrange(
                                "p (b w d) -> p b w d", w=2, d=128)
                            rbc = rtr[:, :].unsqueeze(1).unsqueeze(1).broadcast_to(
                                (128, 4, 2, 128))
                            v.tensor_tensor(onout, otr, rbc,
                                            ALU.mult).then_inc(s_on, 1)
                            v.wait_ge(s_on, 2 * g + 2)
                            v.wait_ge(s_xg, 16 * (g + 1))
                            z1o = xg[:, :].rearrange(
                                "p (d hk jj) -> p d hk jj", hk=HK, jj=4)
                            onin = psum[:, 2048:3072].rearrange(
                                "p (hk d) -> p d hk", hk=HK).unsqueeze(3).broadcast_to(
                                (128, 128, HK, 4))
                            v.tensor_tensor(z1o, z1o, onin,
                                            ALU.add).then_inc(s_z1, 1)
                            v.wait_ge(s_z1, g + 1)
                            v.tensor_reduce(lnt[:, 0:1], xg[:, :], AX.X,
                                            ALU.add).then_inc(s_ldv, 1)
                            v.wait_ge(s_lac, 2 * g + 1)
                            v.wait_ge(s_ldv, 3 * g + 1)
                            v.tensor_scalar(lnt[:, 2:3], lnt[:, 0:1],
                                            1.0 / 4096.0, None, ALU.mult)
                            v.tensor_scalar(lnt[:, 3:4], lnt[:, 1:2],
                                            1.0 / 4096.0, None,
                                            ALU.mult).then_inc(s_hv, 1)
                            v.wait_ge(s_hv, 4 * g + 1)
                            v.tensor_tensor(lnt[:, 6:7], lnt[:, 2:3], lnt[:, 2:3],
                                            ALU.mult).then_inc(s_hv, 1)
                            v.wait_ge(s_hv, 4 * g + 2)
                            v.tensor_tensor(lnt[:, 4:5], lnt[:, 3:4], lnt[:, 6:7],
                                            ALU.subtract).then_inc(s_ldv, 1)
                            v.wait_ge(s_lac, 2 * g + 2)
                            v.tensor_tensor(lnt[:, 6:7], lnt[:, 2:3], lnt[:, 5:6],
                                            ALU.mult).then_inc(s_hv, 1)
                            v.wait_ge(s_hv, 4 * g + 3)
                            v.tensor_scalar(lnt[:, 7:8], lnt[:, 6:7], -1.0, None,
                                            ALU.mult).then_inc(s_hv, 1)
                            v.wait_ge(s_hv, 4 * g + 4)
                            v.nop().then_inc(s_ldv, 1)
                            v.wait_ge(s_yt, 8 * (g + 1))
                            for b8 in range(8):
                                bc = psum[:, b8 * 512:(b8 + 1) * 512].bitcast(BF16)
                                v.tensor_copy(
                                    yT[:, :].rearrange("p (c t) -> p c t", t=2048)
                                    [:, 4 * b8:4 * b8 + 4,
                                     g * 128:g * 128 + 128],
                                    bc[:, 0:512].rearrange(
                                        "p (w t) -> p w t", t=128),
                                ).then_inc(s_ye, 1)

            bc7q = psum[:, 3584:4096].bitcast(BF16)
            with contextlib.ExitStack() as SG:
                gates_all = alloc(SG, "gates_all", [128, 128], F32)
                w_tok = alloc(SG, "w_tok", [128, 128], F32)
                astk = alloc(SG, "astk_sb", [128, 4096], BF16)
                gw1t = alloc(SG, "gw1t_sb", [128, 1024], BF16)
                h_sb = alloc(SG, "h_sb", [128, 32], F32)
                h2_sb = alloc(SG, "h2_sb", [128, 32], F32)
                hg = alloc(SG, "hg", [128, 32], BF16)
                hnt = alloc(SG, "hnt", [32, 128], F32)
                gl = alloc(SG, "gl", [128, 8], F32)
                reps = alloc(SG, "reps", [128, 104], F32)
                rowsc = alloc(SG, "rowsc", [1, 104], F32)
                beats = alloc(SG, "beats", [128, 128], F32)
                oh0 = alloc(SG, "oh0", [128, 128], F32)
                oh1 = alloc(SG, "oh1", [128, 128], F32)
                tmp16 = alloc(SG, "tmp16", [128, 16], F32)
                cmpbuf = alloc(SG, "cmpbuf", [128, 128], F32)
                tprod = alloc(SG, "tprod", [128, 128], F32)
                vv = alloc(SG, "vv", [128, 112], F32)
                cnt_sb = alloc(SG, "cnt_sb", [128, 128], F32)
                cnt2 = alloc(SG, "cnt2", [1, 16], F32)
                alw = alloc(SG, "alw", [1, 16], F32)
                aexp = alloc(SG, "aexp", [1, 256], F32)
                arep = alloc(SG, "arep", [128, 256], F32)
                w01 = alloc(SG, "w01", [128, 256], F32)
                ones16 = alloc(SG, "ones16", [128, 16], F32)
                wexp = alloc(SG, "wexp", [128, 128], F32)
                uwf = alloc(SG, "uwf", [128, 128], BF16)

                SEMS = contextlib.ExitStack()
                s_ld = SEMS.enter_context(nc.semaphore("g_ld"))
                s_ms = SEMS.enter_context(nc.semaphore("g_ms"))
                s_te = SEMS.enter_context(nc.semaphore("g_te"))
                s_dv = SEMS.enter_context(nc.semaphore("g_dv"))
                s_ac = SEMS.enter_context(nc.semaphore("g_ac"))
                s_cc = SEMS.enter_context(nc.semaphore("g_cc"))
                s_cd = SEMS.enter_context(nc.semaphore("g_cd"))
                s_c2 = SEMS.enter_context(nc.semaphore("g_c2"))
                s_hv2 = SEMS.enter_context(nc.semaphore("g_hv"))
                s_ha2 = SEMS.enter_context(nc.semaphore("g_ha"))
                with SEMS, nc.Block("p15") as blk:

                    @blk.gpsimd
                    def _(gp):
                        gp.memset(ones16[:, :], 1.0)
                        gp.memset(cnt_sb[:, :], 0.0).then_inc(s_ms, 1)
                        gp.dma_start(astk[:, :].rearrange(
                            "p (k r) -> p k r", r=128),
                            d_astk[:, :].rearrange("(k p) r -> p k r", p=128)
                        ).then_inc(s_ld, 16)
                        gp.dma_start(gw1t[:, :].rearrange(
                            "p (k c) -> p k c", c=32),
                            d_gw1T[:, :].rearrange("(k p) c -> p k c", p=128)
                        ).then_inc(s_ld, 16)
                        gp.wait_ge(s_cd, 16)
                        gp.collective_compute(
                            "AllReduce", ALU.add,
                            replica_groups=[list(range(8))],
                            ins=[d_ccin.ap().opt()],
                            outs=[d_ccout.ap().opt()],
                        ).then_inc(s_cc)
                        gp.wait_ge(s_cc, 1)
                        gp.dma_start(cnt2[:, :],
                                     d_ccout[0:1, 0:16]).then_inc(s_c2, 16)

                    @blk.scalar
                    def _(sc):
                        for g in range(G):
                            sc.wait_ge(s_dv, 2 + 7 * g + 2)
                            sc.activation(hg[:, :], h_sb[:, :], AF.Square, bias=zvec[:, :], scale=1.0,
                                          accum_out=gl[:, 1:2]).then_inc(s_ac, 1)
                            sc.wait_ge(s_dv, 2 + 7 * g + 3)
                            sc.activation(gl[:, 4:5], gl[:, 4:5], AF.Ln,
                                          bias=evec[:, :],
                                          scale=1.0).then_inc(s_ha2, 1)
                            sc.wait_ge(s_ha2, g + 1)
                            sc.activation(gl[:, 5:6], gl[:, 4:5], AF.Exp,
                                          bias=zvec[:, :], scale=-0.5).then_inc(s_ac, 1)
                            sc.wait_ge(s_dv, 2 + 7 * g + 4)
                            sc.activation(h2_sb[:, :], h_sb[:, :], AF.Identity,
                                          bias=gl[:, 7:8],
                                          scale=gl[:, 5:6]).then_inc(s_ac, 1)
                        sc.wait_ge(s_dv, 115)
                        sc.activation(vv[:, 80:96], vv[:, 32:48], AF.Exp,
                                      bias=zvec[:, :], scale=1.0).then_inc(s_ac, 1)
                        sc.wait_ge(s_dv, 118)
                        sc.dma_start(d_ccin[:, :], cnt_sb[:, :]).then_inc(s_cd, 16)
                        if DEBUG:
                            sc.wait_ge(s_dv, 120)
                            sc.dma_start(d_dbgg[:, :], gates_all[:, :]).then_inc(s_cd, 16)
                            sc.dma_start(d_dbgw[:, :], w_tok[:, :]).then_inc(s_cd, 16)
                            sc.dma_start(d_dbgc[0:1, 0:16], cnt_sb[0:1, 0:16]).then_inc(s_cd, 16)
                            sc.wait_ge(s_cd, 32)
                            sc.dma_start(d_dbgc[0:1, 16:32], cnt2[:, :]).then_inc(s_cd, 16)
                            sc.dma_start(d_dbgc[0:1, 32:48], alw[:, :]).then_inc(s_cd, 16)
                            sc.wait_ge(s_dv, 152)
                            sc.dma_start(d_dbgu[:, :], uwT[:, :]).then_inc(s_cd, 16)
                            sc.wait_ge(s_cd, 32 + 16 * 6)

                    @blk.tensor
                    def _(te):
                        te.wait_ge(s_dv, 1)
                        te.matmul(psum[:, 2560:2664], onesr[:, :], rowsc[:, :],
                                  start=True, stop=True,
                                  skip_group_check=True).then_inc(s_te, 1)
                        te.wait_ge(s_ld, 32)
                        te.wait_ge(s_ms, 1)
                        for g in range(G):
                            if g >= 1:
                                te.wait_ge(s_dv, 2 + 7 * g - 6)
                            for k in range(NKC):
                                mm = te.matmul(
                                    psum[:, 0:32],
                                    yT[:, k * 2048 + g * 128:
                                       k * 2048 + g * 128 + 128],
                                    gw1t[:, k * 32:(k + 1) * 32],
                                    start=(k == 0), stop=(k == NKC - 1),
                                    skip_group_check=True)
                            mm.then_inc(s_te, 1)
                            te.wait_ge(s_dv, 2 + 7 * g + 5)
                            te.matmul(psum[0:32, 512:640], h2_sb[:, :],
                                      idf[:, :],
                                      is_transpose=True, start=True, stop=True,
                                      skip_group_check=True).then_inc(s_te, 1)
                            te.wait_ge(s_dv, 2 + 7 * g + 6)
                            te.matmul(psum[:, 1024:1032], hnt[:, :], gw2t[:, :],
                                      start=True, stop=True,
                                      skip_group_check=True).then_inc(s_te, 1)
                        te.wait_ge(s_dv, 115)
                        te.matmul(psum[:, 1536:1537], oh0[:, :], onesc[:, :],
                                  start=True, stop=True,
                                  skip_group_check=True).then_inc(s_te, 1)
                        te.matmul(psum[:, 2048:2049], oh1[:, :], onesc[:, :],
                                  start=True, stop=True,
                                  skip_group_check=True).then_inc(s_te, 1)
                        te.wait_ge(s_dv, 117)
                        te.matmul(psum[0:1, 2560:2568], cnt_sb[:, 126:127],
                                  selc[:, :], start=True, stop=False,
                                  skip_group_check=True)
                        te.matmul(psum[0:1, 2568:2576], cnt_sb[:, 127:128],
                                  selc[:, :], start=False, stop=True,
                                  skip_group_check=True).then_inc(s_te, 1)
                        te.wait_ge(s_dv, 119)
                        te.matmul(psum[:, 2560:2816], onesr[:, :], aexp[:, :],
                                  start=True, stop=True,
                                  skip_group_check=True).then_inc(s_te, 1)
                        for g in range(G):
                            te.wait_ge(s_dv, 120 + 2 * g)
                            for k in range(NKC):
                                mm = te.matmul(
                                    psum[:, 3072:3200],
                                    yT[:, k * 2048 + g * 128:
                                       k * 2048 + g * 128 + 128],
                                    astk[:, k * 128:(k + 1) * 128],
                                    start=(k == 0), stop=(k == NKC - 1),
                                    skip_group_check=True)
                            mm.then_inc(s_te, 1)
                            te.wait_ge(s_dv, 120 + 2 * g + 1)
                            bc7 = psum[:, 3584:4096].bitcast(BF16)
                            te.matmul(bc7[:, 0:128], uwf[:, :], idb[:, :],
                                      is_transpose=True, start=True, stop=True,
                                      skip_group_check=True).then_inc(s_te, 1)

                    @blk.vector
                    def _(v):
                        hv = [0]

                        def hop(instr):
                            instr.then_inc(s_hv2, 1)
                            hv[0] += 1
                            v.wait_ge(s_hv2, hv[0])

                        v.tensor_copy(rowsc[:, 0:32], gb1r[:, :])
                        v.tensor_copy(rowsc[:, 32:64], lngr[:, :])
                        v.tensor_copy(rowsc[:, 64:96], lnbr[:, :])
                        v.tensor_copy(rowsc[:, 96:104],
                                      gb2r[:, :]).then_inc(s_dv, 1)
                        v.wait_ge(s_te, 1)
                        v.tensor_copy(reps[:, :],
                                      psum[:, 2560:2664]).then_inc(s_dv, 1)
                        v.wait_ge(s_dv, 2)
                        for g in range(G):
                            v.wait_ge(s_te, 2 + 3 * g)
                            hop(v.tensor_tensor(h_sb[:, :], psum[:, 0:32],
                                                reps[:, 0:32], ALU.add))
                            v.nop().then_inc(s_dv, 1)
                            v.tensor_reduce(gl[:, 0:1], h_sb[:, :], AX.X,
                                            ALU.add).then_inc(s_dv, 1)
                            v.wait_ge(s_ac, 3 * g + 1)
                            v.wait_ge(s_dv, 2 + 7 * g + 2)
                            v.tensor_scalar(gl[:, 2:3], gl[:, 0:1], 1.0 / 32.0,
                                            None, ALU.mult)
                            hop(v.tensor_scalar(gl[:, 3:4], gl[:, 1:2],
                                                1.0 / 32.0, None, ALU.mult))
                            hop(v.tensor_tensor(gl[:, 6:7], gl[:, 2:3],
                                                gl[:, 2:3], ALU.mult))
                            v.tensor_tensor(gl[:, 4:5], gl[:, 3:4], gl[:, 6:7],
                                            ALU.subtract).then_inc(s_dv, 1)
                            v.wait_ge(s_ac, 3 * g + 2)
                            hop(v.tensor_tensor(gl[:, 6:7], gl[:, 2:3],
                                                gl[:, 5:6], ALU.mult))
                            hop(v.tensor_scalar(gl[:, 7:8], gl[:, 6:7], -1.0,
                                                None, ALU.mult))
                            v.nop().then_inc(s_dv, 1)
                            v.wait_ge(s_ac, 3 * g + 3)
                            hop(v.tensor_tensor(h2_sb[:, :], h2_sb[:, :],
                                                reps[:, 32:64], ALU.mult))
                            hop(v.tensor_tensor(h2_sb[:, :], h2_sb[:, :],
                                                reps[:, 64:96], ALU.add))
                            hop(v.tensor_scalar(h2_sb[:, :], h2_sb[:, :], 0.0,
                                                None, ALU.max))
                            v.nop().then_inc(s_dv, 1)
                            v.wait_ge(s_te, 3 + 3 * g)
                            v.tensor_copy(hnt[:, :],
                                          psum[0:32, 512:640]).then_inc(s_dv, 1)
                            v.wait_ge(s_te, 4 + 3 * g)
                            v.tensor_tensor(gates_all[:, g * 8:(g + 1) * 8],
                                            psum[:, 1024:1032], reps[:, 96:104],
                                            ALU.add).then_inc(s_dv, 1)
                        v.wait_ge(s_dv, 114)
                        ga = gates_all
                        for e in range(8):
                            for e2 in range(8):
                                op = ALU.is_ge if e2 < e else ALU.is_gt
                                mm = v.tensor_tensor(
                                    cmpbuf[:, e2 * 16:(e2 + 1) * 16],
                                    ga[:, e2::8], ga[:, e::8], op)
                            hop(mm)
                            hop(v.tensor_reduce(
                                beats[:, e::8],
                                cmpbuf[:, :].rearrange(
                                    "p (e2 t) -> p t e2", t=16),
                                AX.X, ALU.add))
                        v.tensor_scalar(oh0[:, :], beats[:, :], 0.0, None,
                                        ALU.is_equal)
                        hop(v.tensor_scalar(oh1[:, :], beats[:, :], 1.0, None,
                                            ALU.is_equal))
                        hop(v.tensor_tensor(tprod[:, :], oh0[:, :], ga[:, :],
                                            ALU.mult))
                        hop(v.tensor_reduce(vv[:, 0:16],
                                            tprod[:, :].rearrange(
                                                "p (g e) -> p g e", e=8),
                                            AX.X, ALU.add))
                        hop(v.tensor_tensor(tprod[:, :], oh1[:, :], ga[:, :],
                                            ALU.mult))
                        hop(v.tensor_reduce(vv[:, 16:32],
                                            tprod[:, :].rearrange(
                                                "p (g e) -> p g e", e=8),
                                            AX.X, ALU.add))
                        v.tensor_tensor(vv[:, 32:48], vv[:, 16:32], vv[:, 0:16],
                                        ALU.subtract).then_inc(s_dv, 1)
                        v.wait_ge(s_ac, 49)
                        hop(v.tensor_scalar(vv[:, 96:112], vv[:, 80:96], 1.0,
                                            None, ALU.add))
                        hop(v.reciprocal(vv[:, 48:64], vv[:, 96:112]))
                        v.tensor_scalar(vv[:, 64:80], vv[:, 48:64], -1.0, 1.0,
                                        ALU.mult, ALU.add).then_inc(s_dv, 1)
                        v.wait_ge(s_te, 51)
                        v.wait_ge(s_ms, 1)
                        v.tensor_copy(cnt_sb[:, 126:127], psum[:, 1536:1537])
                        v.tensor_copy(cnt_sb[:, 127:128],
                                      psum[:, 2048:2049]).then_inc(s_dv, 1)
                        v.wait_ge(s_te, 52)
                        v.tensor_copy(cnt_sb[0:1, 0:16],
                                      psum[0:1, 2560:2576]).then_inc(s_dv, 1)
                        v.wait_ge(s_c2, 16)
                        hop(v.tensor_scalar(alw[:, :], cnt2[:, :], CAP, None,
                                            ALU.is_le))
                        for g in range(G):
                            v.tensor_copy(aexp[:, g * 8:(g + 1) * 8],
                                          alw[:, 0:8])
                            mm = v.tensor_copy(
                                aexp[:, 128 + g * 8:128 + (g + 1) * 8],
                                alw[:, 8:16])
                        hop(mm)
                        v.nop().then_inc(s_dv, 1)
                        v.wait_ge(s_te, 53)
                        hop(v.tensor_copy(arep[:, :], psum[:, 2560:2816]))
                        for g in range(G):
                            v.tensor_scalar(w01[:, g * 8:(g + 1) * 8],
                                            oh0[:, g * 8:(g + 1) * 8],
                                            vv[:, 48 + g:49 + g], None, ALU.mult)
                            mm = v.tensor_scalar(
                                w01[:, 128 + g * 8:128 + (g + 1) * 8],
                                oh1[:, g * 8:(g + 1) * 8],
                                vv[:, 64 + g:65 + g], None, ALU.mult)
                        hop(mm)
                        hop(v.tensor_tensor(w01[:, 0:128], w01[:, 0:128],
                                            arep[:, 0:128], ALU.mult))
                        hop(v.tensor_tensor(w01[:, 128:256], w01[:, 128:256],
                                            arep[:, 128:256], ALU.mult))
                        v.tensor_tensor(w_tok[:, :], w01[:, 0:128],
                                        w01[:, 128:256],
                                        ALU.add).then_inc(s_dv, 1)
                        v.wait_ge(s_dv, 120)
                        for g in range(G):
                            v.wait_ge(s_dv, 120 + 2 * g)
                            for e in range(8):
                                mm = v.tensor_scalar(
                                    wexp[:, e * 16:(e + 1) * 16],
                                    ones16[:, :],
                                    w_tok[:, g * 8 + e:g * 8 + e + 1],
                                    ALPHA, ALU.mult, ALU.mult)
                            hop(mm)
                            v.wait_ge(s_te, 54 + 2 * g)
                            v.tensor_tensor(uwf[:, :], psum[:, 3072:3200],
                                            wexp[:, :],
                                            ALU.mult).then_inc(s_dv, 1)
                            v.wait_ge(s_te, 55 + 2 * g)
                            v.tensor_copy(uwT[:, g * 128:(g + 1) * 128],
                                          bc7q[:, 0:128]).then_inc(s_dv, 1)

            S23 = S1.enter_context(contextlib.ExitStack())
            sums2 = alloc(S23, "sums2", [128, 256], F32)
            sumsq2 = alloc(S23, "sumsq2", [128, 256], F32)
            with contextlib.ExitStack() as S2:
                whalf = [alloc(S2, f"whalf{i}", [128, 8192], BF16) for i in range(2)]
                bslab = [alloc(S2, f"bslab{i}", [128, 256], BF16) for i in range(2)]
                ybig = [alloc(S2, f"ybig{i}", [128, 4096], BF16) for i in range(2)]
                z2big = [alloc(S2, f"z2big{i}", [128, 4096], BF16) for i in range(2)]
                sq_scr = alloc(S2, "sq_scr", [128, 256], BF16)

                with (nc.semaphore("c_wh0") as s_wh0,
                      nc.semaphore("c_wh1") as s_wh1,
                      nc.semaphore("c_bs0") as s_bs0,
                      nc.semaphore("c_bs1") as s_bs1,
                      nc.semaphore("c_yb0") as s_yb0,
                      nc.semaphore("c_yb1") as s_yb1,
                      nc.semaphore("c_mm") as s_mm,
                      nc.semaphore("c_ev") as s_ev,
                      nc.semaphore("c_sq") as s_sq,
                      nc.semaphore("c_zs0") as s_zs0,
                      nc.semaphore("c_zs1") as s_zs1,
                      nc.Block("p2") as blk):

                    @blk.gpsimd
                    def _(gp):
                        for s2 in range(NS2):
                            if s2 >= 2:
                                gp.wait_ge(s_mm, 16 * (s2 - 1))
                            gp.dma_start(
                                whalf[s2 % 2][:, :].rearrange(
                                    "p (k c) -> p k c", c=256),
                                d_wmT[:, s2 * 256:(s2 + 1) * 256].rearrange(
                                    "(k p) c -> p k c", p=128),
                            ).then_inc([s_wh0, s_wh1][s2 % 2], 16)
                            gp.dma_start(
                                bslab[s2 % 2][:, :],
                                d_bstk[:, s2 * 256:(s2 + 1) * 256],
                            ).then_inc([s_bs0, s_bs1][s2 % 2], 16)

                    @blk.sync
                    def _(sy):
                        for s2 in range(NS2):
                            if s2 >= 2:
                                sy.wait_ge(s_ev, 16 * (s2 - 1))
                            sy.dma_start(
                                ybig[s2 % 2][:, :].rearrange(
                                    "p (g c) -> p g c", c=256),
                                d_y[:, s2 * 256:(s2 + 1) * 256].rearrange(
                                    "(g p) c -> p g c", p=128),
                            ).then_inc([s_yb0, s_yb1][s2 % 2], 16)
                            sy.wait_ge(s_ev, 16 * (s2 + 1))
                            sy.wait_ge(s_sq, 16 * (s2 + 1))
                            sy.dma_start(
                                d_z2[:, s2 * 256:(s2 + 1) * 256].rearrange(
                                    "(g p) c -> p g c", p=128),
                                z2big[s2 % 2][:, :].rearrange(
                                    "p (g c) -> p g c", c=256),
                            ).then_inc([s_zs0, s_zs1][s2 % 2], 16)

                    @blk.tensor
                    def _(te):
                        for s2 in range(NS2):
                            te.wait_ge([s_wh0, s_wh1][s2 % 2], 16 * (s2 // 2 + 1))
                            te.wait_ge([s_bs0, s_bs1][s2 % 2], 16 * (s2 // 2 + 1))
                            for g in range(G):
                                idx = s2 * 16 + g
                                if idx >= 8:
                                    te.wait_ge(s_ev, idx - 7)
                                b = (g % 8) * 512
                                for k in range(NKC):
                                    te.matmul(
                                        psum[:, b:b + 256],
                                        yT[:, k * 2048 + g * 128:
                                           k * 2048 + g * 128 + 128],
                                        whalf[s2 % 2][:, k * 256:(k + 1) * 256],
                                        start=(k == 0), stop=False,
                                        skip_group_check=True)
                                mm = te.matmul(
                                    psum[:, b:b + 256],
                                    uwT[:, g * 128:(g + 1) * 128],
                                    bslab[s2 % 2][:, :],
                                    start=False, stop=True,
                                    skip_group_check=True)
                                mm.then_inc(s_mm, 1)

                    @blk.vector
                    def _(v):
                        for s2 in range(NS2):
                            v.wait_ge([s_yb0, s_yb1][s2 % 2], 16 * (s2 // 2 + 1))
                            if s2 >= 2:
                                v.wait_ge([s_zs0, s_zs1][s2 % 2], 16 * (s2 // 2))
                                v.wait_ge(s_sq, 16 * (s2 - 1))
                            for g in range(G):
                                idx = s2 * 16 + g
                                v.wait_ge(s_mm, idx + 1)
                                b = (g % 8) * 512
                                v.scalar_tensor_tensor(
                                    z2big[s2 % 2][:, g * 256:(g + 1) * 256],
                                    psum[:, b:b + 256], 1.0,
                                    ybig[s2 % 2][:, g * 256:(g + 1) * 256],
                                    ALU.mult, ALU.add,
                                    accum_out=sums2[:, g * 16 + s2:
                                                    g * 16 + s2 + 1],
                                ).then_inc(s_ev, 1)

                    @blk.scalar
                    def _(sc):
                        for s2 in range(NS2):
                            for g in range(G):
                                idx = s2 * 16 + g
                                sc.wait_ge(s_ev, idx + 1)
                                if idx >= 1:
                                    sc.wait_ge(s_sq, idx)
                                sc.activation(
                                    sq_scr[:, :],
                                    z2big[s2 % 2][:, g * 256:(g + 1) * 256],
                                    AF.Square, bias=zvec[:, :], scale=1.0,
                                    accum_out=sumsq2[:, g * 16 + s2:
                                                     g * 16 + s2 + 1],
                                ).then_inc(s_sq, 1)
                        sc.wait_ge(s_zs0, 16 * (NS2 // 2))
                        sc.wait_ge(s_zs1, 16 * (NS2 // 2))

            with contextlib.ExitStack() as S3:
                z2g = [alloc(S3, f"z2g{i}", [128, 4096], BF16) for i in range(4)]
                outg = [alloc(S3, f"outg{i}", [128, 4096], F32) for i in range(2)]
                l2 = alloc(S3, "l2", [128, 8], F32)

                SEMS = contextlib.ExitStack()
                s_zl = [SEMS.enter_context(nc.semaphore(f"o_zl{i}"))
                        for i in range(4)]
                s_dbg = SEMS.enter_context(nc.semaphore("o_dbg"))
                s_dv3 = SEMS.enter_context(nc.semaphore("o_dv"))
                s_ac3 = SEMS.enter_context(nc.semaphore("o_ac"))
                s_st3 = [SEMS.enter_context(nc.semaphore(f"o_st{i}"))
                         for i in range(2)]
                s_hv3 = SEMS.enter_context(nc.semaphore("o_hv"))
                s_ha3 = SEMS.enter_context(nc.semaphore("o_ha"))
                with SEMS, nc.Block("p3") as blk:

                    @blk.sync
                    def _(sy):
                        if DEBUG:
                            sy.dma_start(d_dbgqk[:, :], d_qk[:, :]).then_inc(s_dbg, 16)
                            sy.dma_start(d_dbgv[:, :, :], d_vfm[:, :, :]).then_inc(s_dbg, 16)
                            sy.dma_start(d_dbgy[:, :], d_y[:, :]).then_inc(s_dbg, 16)
                            sy.dma_start(d_dbgz[:, :], d_z2[:, :]).then_inc(s_dbg, 16)
                            sy.dma_start(d_dbgwq[:, :, :], d_wqgk[:, :, :]).then_inc(s_dbg, 16)
                            sy.wait_ge(s_dbg, 16 * 5)
                        for g in range(G):
                            if g >= 4:
                                sy.wait_ge(s_ac3, 2 * g - 6)
                            sy.dma_start(z2g[g % 4][:, :],
                                         d_z2[g * 128:(g + 1) * 128, :]
                                         ).then_inc(s_zl[g % 4], 16)
                            sy.wait_ge(s_ac3, 2 * g + 2)
                            sy.dma_start(d_out[g * 128:(g + 1) * 128, :],
                                         outg[g % 2][:, :]).then_inc(s_st3[g % 2], 16)
                        sy.wait_ge(s_st3[0], 16 * 8)
                        sy.wait_ge(s_st3[1], 16 * 8)

                    @blk.vector
                    def _(v):
                        hv = [0]

                        def hop(instr):
                            instr.then_inc(s_hv3, 1)
                            hv[0] += 1
                            v.wait_ge(s_hv3, hv[0])

                        for g in range(G):
                            v.tensor_reduce(l2[:, 0:1],
                                            sums2[:, g * 16:(g + 1) * 16],
                                            AX.X, ALU.add)
                            hop(v.tensor_reduce(l2[:, 1:2],
                                                sumsq2[:, g * 16:(g + 1) * 16],
                                                AX.X, ALU.add))
                            v.tensor_scalar(l2[:, 2:3], l2[:, 0:1],
                                            1.0 / 4096.0, None, ALU.mult)
                            hop(v.tensor_scalar(l2[:, 3:4], l2[:, 1:2],
                                                1.0 / 4096.0, None, ALU.mult))
                            hop(v.tensor_tensor(l2[:, 6:7], l2[:, 2:3],
                                                l2[:, 2:3], ALU.mult))
                            v.tensor_tensor(l2[:, 4:5], l2[:, 3:4], l2[:, 6:7],
                                            ALU.subtract).then_inc(s_dv3, 1)
                            v.wait_ge(s_ac3, 2 * g + 1)
                            hop(v.tensor_tensor(l2[:, 6:7], l2[:, 2:3],
                                                l2[:, 5:6], ALU.mult))
                            hop(v.tensor_scalar(l2[:, 7:8], l2[:, 6:7], -1.0,
                                                None, ALU.mult))
                            v.nop().then_inc(s_dv3, 1)

                    @blk.scalar
                    def _(sc):
                        for g in range(G):
                            sc.wait_ge(s_dv3, 2 * g + 1)
                            sc.activation(l2[:, 4:5], l2[:, 4:5], AF.Ln,
                                          bias=evec[:, :],
                                          scale=1.0).then_inc(s_ha3, 1)
                            sc.wait_ge(s_ha3, g + 1)
                            sc.activation(l2[:, 5:6], l2[:, 4:5], AF.Exp,
                                          bias=zvec[:, :], scale=-0.5).then_inc(s_ac3, 1)
                            sc.wait_ge(s_dv3, 2 * g + 2)
                            sc.wait_ge(s_zl[g % 4], 16 * (g // 4 + 1))
                            if g >= 2:
                                sc.wait_ge(s_st3[g % 2], 16 * (g // 2))
                            sc.activation(outg[g % 2][:, :], z2g[g % 4][:, :],
                                          AF.Identity, bias=l2[:, 7:8],
                                          scale=l2[:, 5:6]).then_inc(s_ac3, 1)

    return nc


_NC_CACHE = [None]


def _bqk4_layout(bq, bk):
    out = np.zeros((36, 1024), dtype=np.float32)
    out[0:4] = bq.reshape(128, 8, 4).transpose(2, 1, 0).reshape(4, 1024)
    out[32] = bk.T.reshape(1024)
    return out


def kernel(**inputs):
    xfull = np.ascontiguousarray(
        np.asarray(inputs["L0_attn_x"], dtype=np.float32).reshape(-1, E))
    Wq = np.asarray(inputs["L0_attn_Wq"], dtype=np.float32)
    bq = np.asarray(inputs["L0_attn_bq"], dtype=np.float32)
    Wk = np.asarray(inputs["L0_attn_Wk"], dtype=np.float32)
    bk = np.asarray(inputs["L0_attn_bk"], dtype=np.float32)
    Wv = np.asarray(inputs["L0_attn_Wv"], dtype=np.float32)
    bv = np.asarray(inputs["L0_attn_bv"], dtype=np.float32)
    W = np.asarray(inputs["L0_ffn_weight"], dtype=np.float32)
    A = np.asarray(inputs["L0_ffn_lora_A"], dtype=np.float32)
    B = np.asarray(inputs["L0_ffn_lora_B"], dtype=np.float32)
    gw1 = np.asarray(inputs["L0_ffn_gw1"], dtype=np.float32)
    gb1 = np.asarray(inputs["L0_ffn_gb1"], dtype=np.float32)
    lng = np.asarray(inputs["L0_ffn_ln_gamma"], dtype=np.float32)
    lnb = np.asarray(inputs["L0_ffn_ln_beta"], dtype=np.float32)
    gw2 = np.asarray(inputs["L0_ffn_gw2"], dtype=np.float32)
    gb2 = np.asarray(inputs["L0_ffn_gb2"], dtype=np.float32)

    shared = {
        "wqr": np.ascontiguousarray(
            Wq.reshape(E, 128, 8, 4).transpose(0, 3, 2, 1)),
        "wkr": np.ascontiguousarray(Wk.transpose(0, 2, 1)),
        "wvr": np.ascontiguousarray(Wv.transpose(0, 2, 1)),
        "bqk4r": _bqk4_layout(bq, bk),
        "ones4r": np.ones((36, 128), dtype=np.float32),
        "bvr": np.ascontiguousarray(bv),
        "wmT": np.ascontiguousarray(W.T),
        "astk": np.ascontiguousarray(
            A.transpose(0, 2, 1).reshape(128, E).T),
        "bstk": np.ascontiguousarray(B.reshape(128, E)),
        "gw1T": np.ascontiguousarray(gw1.T),
        "gw2T": np.ascontiguousarray(gw2.T),
        "gb1r": np.ascontiguousarray(gb1.reshape(1, 32)),
        "lngr": np.ascontiguousarray(lng.reshape(1, 32)),
        "lnbr": np.ascontiguousarray(lnb.reshape(1, 32)),
        "gb2r": np.ascontiguousarray(gb2.reshape(1, 8)),
        "idb": np.eye(128, dtype=BFnp),
        "idf": np.eye(128, dtype=np.float32),
        "onesr": np.ones((1, 128), dtype=np.float32),
        "onesc": np.ones((128, 1), dtype=np.float32),
        "selc": np.ascontiguousarray(
            (np.arange(128)[:, None] % 8 ==
             np.arange(8)[None, :]).astype(np.float32)),
    }

    if _NC_CACHE[0] is None:
        _NC_CACHE[0] = build_nc()
    nc = _NC_CACHE[0]

    in_maps = []
    for c in range(8):
        xs = xfull[c * TC:(c + 1) * TC]
        m = dict(shared)
        m["x"] = np.ascontiguousarray(xs)
        m["xT"] = np.ascontiguousarray(xs.T)
        in_maps.append(m)

    res = run_bass_kernel_spmd(nc, in_maps, core_ids=list(range(8)))
    LAST_RES[0] = res
    out = np.concatenate([np.asarray(r["out"]) for r in res.results], axis=0)
    return out.reshape(np.asarray(inputs["L0_attn_x"]).shape)


LAST_RES = [None]


# version 08
# speedup vs baseline: 1.0047x; 1.0047x over previous
import contextlib
import math
import os
DEBUG = bool(int(os.environ.get("KDEBUG", "0")))

import numpy as np
import ml_dtypes
import concourse.bass as bass
import concourse.mybir as mybir
from concourse.bass_utils import run_bass_kernel_spmd

F32 = mybir.dt.float32
BF16 = mybir.dt.bfloat16
AF = mybir.ActivationFunctionType
ALU = mybir.AluOpType
AX = mybir.AxisListType
BFnp = ml_dtypes.bfloat16

NCORES = 8
TC = 2048
G = 16
E = 4096
HK = 8
NKC = 32
NS2 = 16
CAP = float(int(3.0 * 16384 / 8))
SM_SCALE = 1.0 / math.sqrt(128.0)
CSHIFT = 24.0
LN_EPS = 1e-5
ALPHA = 1.0 / 16.0


def build_nc():
    nc = bass.Bass("TRN2", target_bir_lowering=False)

    d_x = nc.dram_tensor("x", [TC, E], F32, kind="ExternalInput")
    d_xT = nc.dram_tensor("xT", [E, TC], F32, kind="ExternalInput")
    d_wqr = nc.dram_tensor("wqr", [E, 4, HK, 128], F32, kind="ExternalInput")
    d_wkr = nc.dram_tensor("wkr", [E, HK, 128], F32, kind="ExternalInput")
    d_wvr = nc.dram_tensor("wvr", [E, HK, 128], F32, kind="ExternalInput")
    d_bqk4 = nc.dram_tensor("bqk4r", [36, 1024], F32, kind="ExternalInput")
    d_ones4 = nc.dram_tensor("ones4r", [36, 128], F32, kind="ExternalInput")
    d_bvr = nc.dram_tensor("bvr", [128, HK], F32, kind="ExternalInput")
    d_wmT = nc.dram_tensor("wmT", [E, E], F32, kind="ExternalInput")
    d_astk = nc.dram_tensor("astk", [E, 128], F32, kind="ExternalInput")
    d_bstk = nc.dram_tensor("bstk", [128, E], F32, kind="ExternalInput")
    d_gw1T = nc.dram_tensor("gw1T", [E, 32], F32, kind="ExternalInput")
    d_gw2T = nc.dram_tensor("gw2T", [32, 8], F32, kind="ExternalInput")
    d_gb1 = nc.dram_tensor("gb1r", [1, 32], F32, kind="ExternalInput")
    d_lng = nc.dram_tensor("lngr", [1, 32], F32, kind="ExternalInput")
    d_lnb = nc.dram_tensor("lnbr", [1, 32], F32, kind="ExternalInput")
    d_gb2 = nc.dram_tensor("gb2r", [1, 8], F32, kind="ExternalInput")
    d_idb = nc.dram_tensor("idb", [128, 128], BF16, kind="ExternalInput")
    d_idf = nc.dram_tensor("idf", [128, 128], F32, kind="ExternalInput")
    d_ones = nc.dram_tensor("onesr", [1, 128], F32, kind="ExternalInput")
    d_onesc = nc.dram_tensor("onesc", [128, 1], F32, kind="ExternalInput")
    d_selc = nc.dram_tensor("selc", [128, 8], F32, kind="ExternalInput")

    d_out = nc.dram_tensor("out", [TC, E], F32, kind="ExternalOutput")
    if DEBUG:
        d_dbgqk = nc.dram_tensor("dbgqk", [TC, 2048], BF16, kind="ExternalOutput")
        d_dbgv = nc.dram_tensor("dbgv", [128, HK, TC], BF16, kind="ExternalOutput")
        d_dbgy = nc.dram_tensor("dbgy", [TC, E], BF16, kind="ExternalOutput")
        d_dbgz = nc.dram_tensor("dbgz", [TC, E], BF16, kind="ExternalOutput")
        d_dbgg = nc.dram_tensor("dbgg", [128, 128], F32, kind="ExternalOutput")
        d_dbgw = nc.dram_tensor("dbgw", [128, 128], F32, kind="ExternalOutput")
        d_dbgc = nc.dram_tensor("dbgc", [1, 48], F32, kind="ExternalOutput")
        d_dbgo = nc.dram_tensor("dbgo", [128, 1024], F32, kind="ExternalOutput")
        d_dbgr = nc.dram_tensor("dbgr", [128, 128], F32, kind="ExternalOutput")
        d_dbgl = nc.dram_tensor("dbgl", [128, 8], F32, kind="ExternalOutput")
        d_dbgu = nc.dram_tensor("dbgu", [128, 2048], BF16, kind="ExternalOutput")
        d_dbgwq = nc.dram_tensor("dbgwq", [NKC, 128, 2048], BF16, kind="ExternalOutput")

    d_wqgk = nc.dram_tensor("s_wqgk", [NKC, 128, 2048], BF16)
    d_qk = nc.dram_tensor("s_qk", [TC, 2048], BF16)
    d_vfm = nc.dram_tensor("s_vfm", [128, HK, TC], BF16)
    d_y = nc.dram_tensor("s_y", [TC, E], BF16)
    d_z2 = nc.dram_tensor("s_z2", [TC, E], BF16)
    d_ccin = nc.dram_tensor("s_ccin", [128, 128], F32)
    d_ccout = nc.dram_tensor("s_ccout", [128, 128], F32)

    with contextlib.ExitStack() as SO:
        def alloc(stack, name, shape, dt):
            return stack.enter_context(nc.sbuf_tensor(name, shape, dt))

        idb = alloc(SO, "idb_sb", [128, 128], BF16)
        idf = alloc(SO, "idf_sb", [128, 128], F32)
        onesr = alloc(SO, "ones_sb", [1, 128], F32)
        onesc = alloc(SO, "onesc_sb", [128, 1], F32)
        selc = alloc(SO, "selc_sb", [128, 8], F32)
        cvec = alloc(SO, "cvec", [128, 1], F32)
        zvec = alloc(SO, "zvec", [128, 1], F32)
        evec = alloc(SO, "evec", [128, 1], F32)
        bv_sb = alloc(SO, "bv_sb", [128, HK], F32)
        gb1r = alloc(SO, "gb1r_sb", [1, 32], F32)
        lngr = alloc(SO, "lngr_sb", [1, 32], F32)
        lnbr = alloc(SO, "lnbr_sb", [1, 32], F32)
        gb2r = alloc(SO, "gb2r_sb", [1, 8], F32)
        gw2t = alloc(SO, "gw2t_sb", [32, 8], F32)
        psum = SO.enter_context(nc.psum_tensor("ps", [128, 4096], F32))

        with contextlib.ExitStack() as S0:
            xT = alloc(S0, "xT_sb", [128, NKC * 2048], BF16)
            bqk4 = alloc(S0, "bqk4_sb", [36, 1024], F32)
            ones4 = alloc(S0, "ones4_sb", [36, 128], F32)
            qkbias = alloc(S0, "qkbias", [128, 2048], BF16)
            wv_sb = [alloc(S0, f"wv_sb{i}", [128, 4096], BF16) for i in range(2)]
            vtmp = alloc(S0, "vtmp", [128, 2048], BF16)
            reg = alloc(S0, "reg_sb", [128, 24576], BF16)
            wq_ld = [reg[:, i * 4096:(i + 1) * 4096] for i in range(4)]
            wst = [reg[:, 16384 + i * 2048:16384 + (i + 1) * 2048]
                   for i in range(4)]
            wsl = [reg[:, 0:4096], reg[:, 4096:8192]]
            qkev = [reg[:, 8192:9216], reg[:, 9216:10240]]

            SEMS = contextlib.ExitStack()
            s_c = SEMS.enter_context(nc.semaphore("a_c"))
            s_xt = SEMS.enter_context(nc.semaphore("a_xt"))
            s_wl = [SEMS.enter_context(nc.semaphore(f"a_wl{i}"))
                    for i in range(4)]
            s_wk = [SEMS.enter_context(nc.semaphore(f"a_wk{i}"))
                    for i in range(4)]
            s_fd = SEMS.enter_context(nc.semaphore("a_fd"))
            s_st = [SEMS.enter_context(nc.semaphore(f"a_st{i}"))
                    for i in range(4)]
            s_hp = SEMS.enter_context(nc.semaphore("a_hp"))
            s_bq2 = SEMS.enter_context(nc.semaphore("a_bq"))
            s_pe0 = SEMS.enter_context(nc.semaphore("a_pe0"))
            s_wv0 = SEMS.enter_context(nc.semaphore("a_wv0"))
            s_wv1 = SEMS.enter_context(nc.semaphore("a_wv1"))
            s_vm = SEMS.enter_context(nc.semaphore("a_vm"))
            s_vt = SEMS.enter_context(nc.semaphore("a_vt"))
            s_vs = SEMS.enter_context(nc.semaphore("a_vs"))
            s_qw0 = SEMS.enter_context(nc.semaphore("a_qw0"))
            s_qw1 = SEMS.enter_context(nc.semaphore("a_qw1"))
            s_qn = SEMS.enter_context(nc.semaphore("a_qn"))
            s_qe = SEMS.enter_context(nc.semaphore("a_qe"))
            s_qs0 = SEMS.enter_context(nc.semaphore("a_qs0"))
            s_qs1 = SEMS.enter_context(nc.semaphore("a_qs1"))
            with SEMS, nc.Block("p01a") as blk:

                @blk.sync
                def _(sy):
                    consts = [
                        (idb, d_idb[:, :]), (idf, d_idf[:, :]),
                        (onesr, d_ones[:, :]), (onesc, d_onesc[:, :]),
                        (selc, d_selc[:, :]),
                        (bv_sb, d_bvr[:, :]), (gb1r, d_gb1[:, :]),
                        (lngr, d_lng[:, :]), (lnbr, d_lnb[:, :]),
                        (gb2r, d_gb2[:, :]), (gw2t, d_gw2T[:, :]),
                        (bqk4, d_bqk4[:, :]), (ones4, d_ones4[:, :]),
                    ]
                    for dst, src in consts:
                        sy.dma_start(dst[:, :], src).then_inc(s_c, 16)
                    for i in range(4):
                        sy.wait_ge(s_st[i], 16 * 8)
                    for qh in range(8):
                        h = qh % 2
                        for kq in range(8):
                            n = qh * 8 + kq
                            if n >= 2:
                                sy.wait_ge(s_qn, n - 1)
                            sy.dma_start(
                                wsl[n % 2].rearrange("p (k c) -> p k c", c=1024),
                                d_wqgk[4 * kq:4 * kq + 4, :,
                                       h * 1024:(h + 1) * 1024].rearrange(
                                    "k p c -> p k c"),
                            ).then_inc([s_qw0, s_qw1][n % 2], 16)

                @blk.gpsimd
                def _(gp):
                    gp.memset(cvec[:, :], -CSHIFT)
                    gp.memset(zvec[:, :], 0.0)
                    gp.memset(evec[:, :], LN_EPS)
                    src = d_xT[:, :].rearrange("(k p) t -> p k t", p=128)
                    for q in range(4):
                        gp.dma_start(
                            xT[:, q * 8 * 2048:(q + 1) * 8 * 2048].rearrange(
                                "p (k t) -> p k t", t=2048),
                            src[:, q * 8:(q + 1) * 8, :],
                        ).then_inc(s_xt, 16)
                    for hk in range(2):
                        gp.dma_start(
                            wv_sb[hk][:, :].rearrange("p (k d) -> p k d", d=128),
                            d_wvr[:, hk, :].rearrange("(k p) d -> p k d", p=128),
                        ).then_inc([s_wv0, s_wv1][hk], 16)
                    wsched = {4: 2, 10: 3, 14: 4, 18: 5, 22: 6, 26: 7}
                    for k in range(NKC):
                        if k >= 4:
                            gp.wait_ge(s_st[k % 4], 16 * (k // 4))
                        gp.dma_start(
                            wq_ld[k % 4],
                            d_wqr[:, :, :, :].rearrange(
                                "(k p) a b c -> k p (a b c)", p=128)[k],
                        ).then_inc(s_wl[k % 4], 16)
                        gp.dma_start(
                            wst[k % 4][:, 1024:2048],
                            d_wkr[:, :, :].rearrange(
                                "(k p) a b -> k p (a b)", p=128)[k],
                        ).then_inc(s_wk[k % 4], 16)
                        if k in wsched:
                            hk = wsched[k]
                            gp.wait_ge(s_vm, hk - 1)
                            gp.dma_start(
                                wv_sb[hk % 2][:, :].rearrange(
                                    "p (k d) -> p k d", d=128),
                                d_wvr[:, hk, :].rearrange(
                                    "(k p) d -> p k d", p=128),
                            ).then_inc([s_wv0, s_wv1][hk % 2], 16)

                @blk.tensor
                def _(te):
                    te.wait_ge(s_c, 16 * 13)
                    for half in range(2):
                        for w in range(2):
                            mm = te.matmul(
                                psum[:, half * 1024 + w * 512:
                                     half * 1024 + (w + 1) * 512],
                                ones4[32 * half:32 * half + 4, :],
                                bqk4[32 * half:32 * half + 4,
                                     w * 512:(w + 1) * 512],
                                start=True, stop=True,
                                skip_group_check=True)
                            mm.then_inc(s_pe0, 1)
                    te.wait_ge(s_xt, 64)
                    te.wait_ge(s_bq2, 1)
                    for hk in range(HK):
                        te.wait_ge([s_wv0, s_wv1][hk % 2],
                                   16 * (hk // 2 + 1))
                        if hk >= 2:
                            te.wait_ge(s_vt, hk - 1)
                        base = 0 if hk % 2 == 0 else 4
                        for k in range(NKC):
                            for w in range(4):
                                mm = te.matmul(
                                    psum[:, (base + w) * 512:(base + w + 1) * 512],
                                    wv_sb[hk % 2][:, k * 128:(k + 1) * 128],
                                    xT[:, k * 2048 + w * 512:
                                       k * 2048 + (w + 1) * 512],
                                    start=(k == 0), stop=(k == NKC - 1),
                                    skip_group_check=True)
                        mm.then_inc(s_vm, 1)
                    te.wait_ge(s_vt, 8)
                    for qh in range(8):
                        q, h = qh // 2, qh % 2
                        if qh >= 1:
                            te.wait_ge(s_qe, 4 * qh)
                        for kq in range(8):
                            nn = qh * 8 + kq
                            te.wait_ge([s_qw0, s_qw1][nn % 2],
                                       16 * (nn // 2 + 1))
                            for kk in range(4):
                                k = 4 * kq + kk
                                for sg in range(4):
                                    tok0 = (q * 4 + sg) * 128
                                    for i in range(2):
                                        mm = te.matmul(
                                            psum[:, sg * 1024 + i * 512:
                                                 sg * 1024 + (i + 1) * 512],
                                            xT[:, k * 2048 + tok0:
                                               k * 2048 + tok0 + 128],
                                            wsl[(qh * 8 + kq) % 2][
                                                :, kk * 1024 + i * 512:
                                                kk * 1024 + (i + 1) * 512],
                                            start=(k == 0), stop=(k == NKC - 1),
                                            skip_group_check=True)
                            mm.then_inc(s_qn, 1)

                @blk.vector
                def _(v):
                    hv = [0]

                    def hop(instr):
                        instr.then_inc(s_hp, 1)
                        hv[0] += 1
                        v.wait_ge(s_hp, hv[0])

                    def fold(k):
                        v.wait_ge(s_wl[k % 4], 16 * (k // 4 + 1))
                        if k >= 4:
                            v.wait_ge(s_st[k % 4], 16 * (k // 4))
                        b = wq_ld[k % 4]
                        w = wst[k % 4][:, 0:1024]
                        hop(v.tensor_tensor(w, b[:, 0:1024],
                                            b[:, 1024:2048], ALU.add))
                        hop(v.tensor_tensor(w, w, b[:, 2048:3072], ALU.add))
                        v.tensor_tensor(w, w, b[:, 3072:4096],
                                        ALU.add).then_inc(s_fd, 1)

                    def vevac(hk):
                        v.wait_ge(s_vm, hk + 1)
                        if hk >= 1:
                            v.wait_ge(s_vs, 16 * hk)
                        base = 0 if hk % 2 == 0 else 4
                        v.tensor_scalar(vtmp[:, :],
                                        psum[:, base * 512:(base + 4) * 512],
                                        bv_sb[:, hk:hk + 1], None,
                                        ALU.add).then_inc(s_vt, 1)

                    sched = {2: -1, 8: 0, 11: 1, 14: 2, 18: 3, 22: 4,
                             26: 5, 29: 6}
                    for k in range(NKC):
                        fold(k)
                        if k in sched:
                            if sched[k] == -1:
                                v.wait_ge(s_pe0, 4)
                                for w in range(4):
                                    cp = v.tensor_copy(
                                        qkbias[:, w * 512:(w + 1) * 512],
                                        psum[:, w * 512:(w + 1) * 512])
                                cp.then_inc(s_bq2, 1)
                            else:
                                vevac(sched[k])
                    vevac(7)
                    for qh in range(8):
                        v.wait_ge(s_qn, 8 * (qh + 1))
                        for sg in range(4):
                            n = qh * 4 + sg
                            if n >= 2:
                                v.wait_ge([s_qs0, s_qs1][n % 2], 16 * (n // 2))
                            v.tensor_tensor(
                                qkev[n % 2],
                                psum[:, sg * 1024:(sg + 1) * 1024],
                                qkbias[:, (qh % 2) * 1024:(qh % 2 + 1) * 1024],
                                ALU.add).then_inc(s_qe, 1)

                @blk.scalar
                def _(sc):
                    vsched = {9: 0, 12: 1, 16: 2, 20: 3, 24: 4, 27: 5, 30: 6}
                    for k in range(NKC):
                        sc.wait_ge(s_fd, k + 1)
                        sc.wait_ge(s_wk[k % 4], 16 * (k // 4 + 1))
                        sc.dma_start(d_wqgk[k], wst[k % 4][:, :]).then_inc(
                            s_st[k % 4], 16)
                        if k in vsched:
                            hk = vsched[k]
                            sc.wait_ge(s_vt, hk + 1)
                            sc.dma_start(d_vfm[:, hk, :],
                                         vtmp[:, :]).then_inc(s_vs, 16)
                    sc.wait_ge(s_vt, 8)
                    sc.dma_start(d_vfm[:, 7, :],
                                 vtmp[:, :]).then_inc(s_vs, 16)
                    sc.wait_ge(s_vs, 16 * 8)
                    for qh in range(8):
                        q, h = qh // 2, qh % 2
                        for sg in range(4):
                            n = qh * 4 + sg
                            sc.wait_ge(s_qe, n + 1)
                            r0 = (q * 4 + sg) * 128
                            sc.dma_start(
                                d_qk[r0:r0 + 128, h * 1024:(h + 1) * 1024],
                                qkev[n % 2]).then_inc(
                                [s_qs0, s_qs1][n % 2], 16)
                    sc.wait_ge(s_qs0, 16 * 16)
                    sc.wait_ge(s_qs1, 16 * 16)

        with contextlib.ExitStack() as S1:
            yT = alloc(S1, "yT", [128, NKC * 2048], BF16)
            uwT = alloc(S1, "uwT", [128, 2048], BF16)

            with contextlib.ExitStack() as SB:
                qg_pack = [alloc(SB, f"qgp{i}", [128, 4096], BF16) for i in range(2)]
                k_pack = [alloc(SB, f"kp{i}", [128, 4096], BF16) for i in range(2)]
                vbuf = alloc(SB, "vbuf", [128, 1152], BF16)
                pprime = alloc(SB, "pprime", [128, 4096], BF16)
                o_all = alloc(SB, "o_all", [128, 1024], F32)
                xg = alloc(SB, "xg", [128, 4096], F32)
                ytok = alloc(SB, "ytok", [128, 4096], BF16)
                rall = alloc(SB, "rall", [128, 128], F32)
                rtr = rall
                lnt = alloc(SB, "lnt", [128, 16], F32)

                SEMS = contextlib.ExitStack()
                s_pk = [SEMS.enter_context(nc.semaphore("b_pk0")),
                        SEMS.enter_context(nc.semaphore("b_pk1"))]
                s_vb = SEMS.enter_context(nc.semaphore("b_vb"))
                s_mb = SEMS.enter_context(nc.semaphore("b_mb"))
                s_xg = SEMS.enter_context(nc.semaphore("b_xg"))
                s_sc = SEMS.enter_context(nc.semaphore("b_sc"))
                s_ex = SEMS.enter_context(nc.semaphore("b_ex"))
                s_av = SEMS.enter_context(nc.semaphore("b_av"))
                s_rc = SEMS.enter_context(nc.semaphore("b_rc"))
                s_oe = SEMS.enter_context(nc.semaphore("b_oe"))
                s_ot = SEMS.enter_context(nc.semaphore("b_ot"))
                s_on = SEMS.enter_context(nc.semaphore("b_on"))
                s_z1 = SEMS.enter_context(nc.semaphore("b_z1"))
                s_ldv = SEMS.enter_context(nc.semaphore("b_ldv"))
                s_lac = SEMS.enter_context(nc.semaphore("b_lac"))
                s_y = SEMS.enter_context(nc.semaphore("b_y"))
                s_ys = SEMS.enter_context(nc.semaphore("b_ys"))
                s_yt = SEMS.enter_context(nc.semaphore("b_yt"))
                s_ye = SEMS.enter_context(nc.semaphore("b_ye"))
                s_hv = SEMS.enter_context(nc.semaphore("b_hv"))
                s_ha = SEMS.enter_context(nc.semaphore("b_ha"))
                with SEMS, nc.Block("p1b") as blk:

                    @blk.gpsimd
                    def _(gp):
                        gp.memset(vbuf[:, 1024:1152], 1.0).then_inc(s_mb, 1)
                        for g in range(G):
                            if g >= 2:
                                gp.wait_ge(s_sc, (g - 1) * 8)
                            for half, pk in ((0, qg_pack), (1, k_pack)):
                                src = d_qk[g * 128:(g + 1) * 128,
                                           half * 1024:(half + 1) * 1024].rearrange(
                                    "(p j) (hk d) -> j hk p d", j=4, hk=HK)
                                for j in range(4):
                                    gp.dma_start(
                                        pk[g % 2][32 * j:32 * j + HK, :].rearrange(
                                            "hk (p d) -> hk p d", p=32),
                                        src[j],
                                    ).then_inc(s_pk[g % 2], 16)

                    @blk.sync
                    def _(sy):
                        for g in range(G):
                            if g >= 1:
                                sy.wait_ge(s_av, g * 8)
                            sy.dma_start(
                                vbuf[:, 0:1024].rearrange(
                                    "p (hk t) -> p hk t", hk=HK),
                                d_vfm[:, :, g * 128:(g + 1) * 128],
                            ).then_inc(s_vb, 16)
                            if g >= 1:
                                sy.wait_ge(s_y, g)
                            sy.dma_start(xg[:, :],
                                         d_x[g * 128:(g + 1) * 128, :]
                                         ).then_inc(s_xg, 16)
                            sy.wait_ge(s_y, g + 1)
                            sy.dma_start(d_y[g * 128:(g + 1) * 128, :],
                                         ytok[:, :]).then_inc(s_ys, 16)
                        sy.wait_ge(s_ys, 16 * G)
                        if DEBUG:
                            sy.wait_ge(s_ot, G)
                            sy.dma_start(d_dbgo[:, :], o_all[:, :]).then_inc(s_ys, 16)
                            sy.dma_start(d_dbgr[:, :], rall[:, :]).then_inc(s_ys, 16)
                            sy.wait_ge(s_y, G)
                            sy.dma_start(d_dbgl[:, :], lnt[:, :]).then_inc(s_ys, 16)
                            sy.wait_ge(s_ys, 16 * (G + 3))

                    @blk.tensor
                    def _(te):
                        for g in range(G):
                            fi0 = g * 8
                            te.wait_ge(s_pk[g % 2], 16 * 8 * (g // 2 + 1))
                            te.wait_ge(s_vb, 16 * (g + 1))
                            te.wait_ge(s_mb, 1)

                            def scores(f):
                                fi = fi0 + f
                                if g >= 1 and f == 0:
                                    te.wait_ge(s_ye, 8 * g - 4)
                                if g >= 1 and f == 1:
                                    te.wait_ge(s_ye, 8 * g)
                                if fi >= 2:
                                    te.wait_ge(s_oe, fi - 1)
                                sp = (f % 2) * 2048
                                for i in range(16):
                                    t = 16 * f + i
                                    p, j = t // 4, t % 4
                                    w = i // 4
                                    mm = te.matmul(
                                        psum[:, sp + j * 512 + w * 128:
                                             sp + j * 512 + (w + 1) * 128],
                                        k_pack[g % 2][32 * j:32 * j + 8,
                                                      p * 128:(p + 1) * 128],
                                        qg_pack[g % 2][32 * j:32 * j + 8,
                                                       p * 128:(p + 1) * 128],
                                        start=(w == 0), stop=(w == 3),
                                        tile_position=(32 * j, 0),
                                        skip_group_check=True)
                                mm.then_inc(s_sc, 1)

                            def av(f):
                                fi = fi0 + f
                                sp = (f % 2) * 2048
                                order = [i for i in range(16) if i % 4 < 2] \
                                    + [i for i in range(16) if i % 4 >= 2]
                                te.wait_ge(s_ex, 2 * fi + 1)
                                for n, i in enumerate(order):
                                    if n == 8:
                                        te.wait_ge(s_ex, 2 * fi + 2)
                                    t = 16 * f + i
                                    p, j = t // 4, t % 4
                                    w = i // 4
                                    mm = te.matmul(
                                        psum[:, sp + i * 12: sp + i * 12 + 9],
                                        pprime[:, sp + j * 512 + w * 128:
                                               sp + j * 512 + (w + 1) * 128],
                                        vbuf[:, t::128],
                                        start=(n == 0), stop=(n == 15),
                                        skip_group_check=True)
                                mm.then_inc(s_av, 1)

                            scores(0)
                            scores(1)
                            for f in range(8):
                                av(f)
                                if f + 2 < 8:
                                    scores(f + 2)
                            te.wait_ge(s_oe, fi0 + 8)
                            if g >= 1:
                                te.wait_ge(s_z1, g)
                            for hk in range(HK):
                                b, w = hk // 2, hk % 2
                                mm = te.matmul(
                                    psum[:, b * 512 + w * 128:
                                         b * 512 + (w + 1) * 128],
                                    o_all[:, hk * 128:(hk + 1) * 128],
                                    idf[:, :], is_transpose=True,
                                    start=(w == 0),
                                    stop=(w == 1 and b > 0),
                                    skip_group_check=True)
                            te.matmul(psum[:, 256:384], rall[:, :], idf[:, :],
                                      is_transpose=True, start=False, stop=True,
                                      skip_group_check=True).then_inc(s_ot, 1)
                            te.wait_ge(s_y, g + 1)
                            te.wait_ge(s_z1, g + 1)
                            for b8 in range(8):
                                bc = psum[:, b8 * 512:(b8 + 1) * 512].bitcast(BF16)
                                for w in range(4):
                                    c = b8 * 4 + w
                                    mm = te.matmul(
                                        bc[:, w * 128:(w + 1) * 128],
                                        ytok[:, c * 128:(c + 1) * 128],
                                        idb[:, :], is_transpose=True,
                                        start=(w == 0), stop=(w == 3),
                                        skip_group_check=True)
                                mm.then_inc(s_yt, 1)

                    @blk.scalar
                    def _(sc):
                        for g in range(G):
                            fi0 = g * 8
                            for f in range(8):
                                fi = fi0 + f
                                sc.wait_ge(s_sc, fi + 1)
                                if fi >= 2:
                                    sc.wait_ge(s_av, fi - 1)
                                sp = (f % 2) * 2048
                                for hh in range(2):
                                    sc.activation(
                                        pprime[:, sp + hh * 1024:
                                               sp + (hh + 1) * 1024],
                                        psum[:, sp + hh * 1024:
                                             sp + (hh + 1) * 1024], AF.Exp,
                                        bias=cvec[:, :],
                                        scale=SM_SCALE).then_inc(s_ex, 1)
                            sc.wait_ge(s_z1, g + 1)
                            if g >= 1:
                                sc.wait_ge(s_ys, 16 * g)
                                sc.wait_ge(s_yt, 8 * g)
                            sc.activation(ytok[:, :], xg[:, :], AF.Square, bias=zvec[:, :], scale=1.0,
                                          accum_out=lnt[:, 1:2]).then_inc(s_lac, 1)
                            sc.wait_ge(s_ldv, 3 * g + 2)
                            sc.activation(lnt[:, 4:5], lnt[:, 4:5], AF.Ln,
                                          bias=evec[:, :],
                                          scale=1.0).then_inc(s_ha, 1)
                            sc.wait_ge(s_ha, g + 1)
                            sc.activation(lnt[:, 5:6], lnt[:, 4:5], AF.Exp,
                                          bias=zvec[:, :], scale=-0.5).then_inc(s_lac, 1)
                            sc.wait_ge(s_ldv, 3 * g + 3)
                            sc.activation(ytok[:, :], xg[:, :], AF.Identity,
                                          bias=lnt[:, 7:8],
                                          scale=lnt[:, 5:6]).then_inc(s_y, 1)

                    @blk.vector
                    def _(v):
                        for g in range(G):
                            fi0 = g * 8
                            for f in range(8):
                                fi = fi0 + f
                                v.wait_ge(s_av, fi + 1)
                                if f == 0 and g >= 1:
                                    v.wait_ge(s_ot, g)
                                sp = (f % 2) * 2048
                                v.reciprocal(rall[:, 16 * f:16 * f + 16],
                                             psum[:, sp + 8: sp + 192:12]
                                             ).then_inc(s_rc, 1)
                                v.tensor_copy(
                                    o_all[:, :].rearrange(
                                        "p (hk t) -> p t hk", hk=HK)
                                    [:, 16 * f:16 * f + 16, :],
                                    psum[:, sp:sp + 192].rearrange(
                                        "p (t c) -> p t c", c=12)[:, :, 0:8],
                                ).then_inc(s_oe, 1)
                            v.wait_ge(s_ot, g + 1)
                            v.tensor_copy(rtr[:, :],
                                          psum[:, 256:384]).then_inc(s_on, 1)
                            v.wait_ge(s_on, 2 * g + 1)
                            otr = psum[:, 0:2048].rearrange(
                                "p (b c) -> p b c", c=512)[:, :, 0:256].rearrange(
                                "p b (w d) -> p b w d", d=128)
                            onout = psum[:, 2048:3072].rearrange(
                                "p (b w d) -> p b w d", w=2, d=128)
                            rbc = rtr[:, :].unsqueeze(1).unsqueeze(1).broadcast_to(
                                (128, 4, 2, 128))
                            v.tensor_tensor(onout, otr, rbc,
                                            ALU.mult).then_inc(s_on, 1)
                            v.wait_ge(s_on, 2 * g + 2)
                            v.wait_ge(s_xg, 16 * (g + 1))
                            z1o = xg[:, :].rearrange(
                                "p (d hk jj) -> p d hk jj", hk=HK, jj=4)
                            onin = psum[:, 2048:3072].rearrange(
                                "p (hk d) -> p d hk", hk=HK).unsqueeze(3).broadcast_to(
                                (128, 128, HK, 4))
                            v.tensor_tensor(z1o, z1o, onin,
                                            ALU.add).then_inc(s_z1, 1)
                            v.wait_ge(s_z1, g + 1)
                            v.tensor_reduce(lnt[:, 0:1], xg[:, :], AX.X,
                                            ALU.add).then_inc(s_ldv, 1)
                            v.wait_ge(s_lac, 2 * g + 1)
                            v.wait_ge(s_ldv, 3 * g + 1)
                            v.tensor_scalar(lnt[:, 2:3], lnt[:, 0:1],
                                            1.0 / 4096.0, None, ALU.mult)
                            v.tensor_scalar(lnt[:, 3:4], lnt[:, 1:2],
                                            1.0 / 4096.0, None,
                                            ALU.mult).then_inc(s_hv, 1)
                            v.wait_ge(s_hv, 4 * g + 1)
                            v.tensor_tensor(lnt[:, 6:7], lnt[:, 2:3], lnt[:, 2:3],
                                            ALU.mult).then_inc(s_hv, 1)
                            v.wait_ge(s_hv, 4 * g + 2)
                            v.tensor_tensor(lnt[:, 4:5], lnt[:, 3:4], lnt[:, 6:7],
                                            ALU.subtract).then_inc(s_ldv, 1)
                            v.wait_ge(s_lac, 2 * g + 2)
                            v.tensor_tensor(lnt[:, 6:7], lnt[:, 2:3], lnt[:, 5:6],
                                            ALU.mult).then_inc(s_hv, 1)
                            v.wait_ge(s_hv, 4 * g + 3)
                            v.tensor_scalar(lnt[:, 7:8], lnt[:, 6:7], -1.0, None,
                                            ALU.mult).then_inc(s_hv, 1)
                            v.wait_ge(s_hv, 4 * g + 4)
                            v.nop().then_inc(s_ldv, 1)
                            v.wait_ge(s_yt, 8 * (g + 1))
                            for b8 in range(8):
                                bc = psum[:, b8 * 512:(b8 + 1) * 512].bitcast(BF16)
                                v.tensor_copy(
                                    yT[:, :].rearrange("p (c t) -> p c t", t=2048)
                                    [:, 4 * b8:4 * b8 + 4,
                                     g * 128:g * 128 + 128],
                                    bc[:, 0:512].rearrange(
                                        "p (w t) -> p w t", t=128),
                                ).then_inc(s_ye, 1)

            bc7q = psum[:, 3584:4096].bitcast(BF16)
            with contextlib.ExitStack() as SG:
                gates_all = alloc(SG, "gates_all", [128, 128], F32)
                w_tok = alloc(SG, "w_tok", [128, 128], F32)
                astk = alloc(SG, "astk_sb", [128, 4096], BF16)
                gw1t = alloc(SG, "gw1t_sb", [128, 1024], BF16)
                h_sb = alloc(SG, "h_sb", [128, 32], F32)
                h2_sb = alloc(SG, "h2_sb", [128, 32], F32)
                hg = alloc(SG, "hg", [128, 32], BF16)
                hnt = alloc(SG, "hnt", [32, 128], F32)
                gl = alloc(SG, "gl", [128, 8], F32)
                reps = alloc(SG, "reps", [128, 104], F32)
                rowsc = alloc(SG, "rowsc", [1, 104], F32)
                beats = alloc(SG, "beats", [128, 128], F32)
                oh0 = alloc(SG, "oh0", [128, 128], F32)
                oh1 = alloc(SG, "oh1", [128, 128], F32)
                tmp16 = alloc(SG, "tmp16", [128, 16], F32)
                cmpbuf = alloc(SG, "cmpbuf", [128, 128], F32)
                tprod = alloc(SG, "tprod", [128, 128], F32)
                vv = alloc(SG, "vv", [128, 112], F32)
                cnt_sb = alloc(SG, "cnt_sb", [128, 128], F32)
                cnt2 = alloc(SG, "cnt2", [1, 16], F32)
                alw = alloc(SG, "alw", [1, 16], F32)
                aexp = alloc(SG, "aexp", [1, 256], F32)
                arep = alloc(SG, "arep", [128, 256], F32)
                w01 = alloc(SG, "w01", [128, 256], F32)
                ones16 = alloc(SG, "ones16", [128, 16], F32)
                wexp = alloc(SG, "wexp", [128, 128], F32)
                uwf = alloc(SG, "uwf", [128, 128], BF16)

                SEMS = contextlib.ExitStack()
                s_ld = SEMS.enter_context(nc.semaphore("g_ld"))
                s_ms = SEMS.enter_context(nc.semaphore("g_ms"))
                s_te = SEMS.enter_context(nc.semaphore("g_te"))
                s_dv = SEMS.enter_context(nc.semaphore("g_dv"))
                s_ac = SEMS.enter_context(nc.semaphore("g_ac"))
                s_cc = SEMS.enter_context(nc.semaphore("g_cc"))
                s_cd = SEMS.enter_context(nc.semaphore("g_cd"))
                s_c2 = SEMS.enter_context(nc.semaphore("g_c2"))
                s_hv2 = SEMS.enter_context(nc.semaphore("g_hv"))
                s_ha2 = SEMS.enter_context(nc.semaphore("g_ha"))
                with SEMS, nc.Block("p15") as blk:

                    @blk.gpsimd
                    def _(gp):
                        gp.memset(ones16[:, :], 1.0)
                        gp.memset(cnt_sb[:, :], 0.0).then_inc(s_ms, 1)
                        gp.dma_start(astk[:, :].rearrange(
                            "p (k r) -> p k r", r=128),
                            d_astk[:, :].rearrange("(k p) r -> p k r", p=128)
                        ).then_inc(s_ld, 16)
                        gp.dma_start(gw1t[:, :].rearrange(
                            "p (k c) -> p k c", c=32),
                            d_gw1T[:, :].rearrange("(k p) c -> p k c", p=128)
                        ).then_inc(s_ld, 16)
                        gp.wait_ge(s_cd, 16)
                        gp.collective_compute(
                            "AllReduce", ALU.add,
                            replica_groups=[list(range(8))],
                            ins=[d_ccin.ap().opt()],
                            outs=[d_ccout.ap().opt()],
                        ).then_inc(s_cc)
                        gp.wait_ge(s_cc, 1)
                        gp.dma_start(cnt2[:, :],
                                     d_ccout[0:1, 0:16]).then_inc(s_c2, 16)

                    @blk.scalar
                    def _(sc):
                        for g in range(G):
                            sc.wait_ge(s_dv, 2 + 7 * g + 2)
                            sc.activation(hg[:, :], h_sb[:, :], AF.Square, bias=zvec[:, :], scale=1.0,
                                          accum_out=gl[:, 1:2]).then_inc(s_ac, 1)
                            sc.wait_ge(s_dv, 2 + 7 * g + 3)
                            sc.activation(gl[:, 4:5], gl[:, 4:5], AF.Ln,
                                          bias=evec[:, :],
                                          scale=1.0).then_inc(s_ha2, 1)
                            sc.wait_ge(s_ha2, g + 1)
                            sc.activation(gl[:, 5:6], gl[:, 4:5], AF.Exp,
                                          bias=zvec[:, :], scale=-0.5).then_inc(s_ac, 1)
                            sc.wait_ge(s_dv, 2 + 7 * g + 4)
                            sc.activation(h2_sb[:, :], h_sb[:, :], AF.Identity,
                                          bias=gl[:, 7:8],
                                          scale=gl[:, 5:6]).then_inc(s_ac, 1)
                        sc.wait_ge(s_dv, 115)
                        sc.activation(vv[:, 80:96], vv[:, 32:48], AF.Exp,
                                      bias=zvec[:, :], scale=1.0).then_inc(s_ac, 1)
                        sc.wait_ge(s_dv, 118)
                        sc.dma_start(d_ccin[:, :], cnt_sb[:, :]).then_inc(s_cd, 16)
                        if DEBUG:
                            sc.wait_ge(s_dv, 120)
                            sc.dma_start(d_dbgg[:, :], gates_all[:, :]).then_inc(s_cd, 16)
                            sc.dma_start(d_dbgw[:, :], w_tok[:, :]).then_inc(s_cd, 16)
                            sc.dma_start(d_dbgc[0:1, 0:16], cnt_sb[0:1, 0:16]).then_inc(s_cd, 16)
                            sc.wait_ge(s_cd, 32)
                            sc.dma_start(d_dbgc[0:1, 16:32], cnt2[:, :]).then_inc(s_cd, 16)
                            sc.dma_start(d_dbgc[0:1, 32:48], alw[:, :]).then_inc(s_cd, 16)
                            sc.wait_ge(s_dv, 152)
                            sc.dma_start(d_dbgu[:, :], uwT[:, :]).then_inc(s_cd, 16)
                            sc.wait_ge(s_cd, 32 + 16 * 6)

                    @blk.tensor
                    def _(te):
                        te.wait_ge(s_dv, 1)
                        te.matmul(psum[:, 2560:2664], onesr[:, :], rowsc[:, :],
                                  start=True, stop=True,
                                  skip_group_check=True).then_inc(s_te, 1)
                        te.wait_ge(s_ld, 32)
                        te.wait_ge(s_ms, 1)
                        for g in range(G):
                            if g >= 1:
                                te.wait_ge(s_dv, 2 + 7 * g - 6)
                            for k in range(NKC):
                                mm = te.matmul(
                                    psum[:, 0:32],
                                    yT[:, k * 2048 + g * 128:
                                       k * 2048 + g * 128 + 128],
                                    gw1t[:, k * 32:(k + 1) * 32],
                                    start=(k == 0), stop=(k == NKC - 1),
                                    skip_group_check=True)
                            mm.then_inc(s_te, 1)
                            te.wait_ge(s_dv, 2 + 7 * g + 5)
                            te.matmul(psum[0:32, 512:640], h2_sb[:, :],
                                      idf[:, :],
                                      is_transpose=True, start=True, stop=True,
                                      skip_group_check=True).then_inc(s_te, 1)
                            te.wait_ge(s_dv, 2 + 7 * g + 6)
                            te.matmul(psum[:, 1024:1032], hnt[:, :], gw2t[:, :],
                                      start=True, stop=True,
                                      skip_group_check=True).then_inc(s_te, 1)
                        te.wait_ge(s_dv, 115)
                        te.matmul(psum[:, 1536:1537], oh0[:, :], onesc[:, :],
                                  start=True, stop=True,
                                  skip_group_check=True).then_inc(s_te, 1)
                        te.matmul(psum[:, 2048:2049], oh1[:, :], onesc[:, :],
                                  start=True, stop=True,
                                  skip_group_check=True).then_inc(s_te, 1)
                        te.wait_ge(s_dv, 117)
                        te.matmul(psum[0:1, 2560:2568], cnt_sb[:, 126:127],
                                  selc[:, :], start=True, stop=False,
                                  skip_group_check=True)
                        te.matmul(psum[0:1, 2568:2576], cnt_sb[:, 127:128],
                                  selc[:, :], start=False, stop=True,
                                  skip_group_check=True).then_inc(s_te, 1)
                        te.wait_ge(s_dv, 119)
                        te.matmul(psum[:, 2560:2816], onesr[:, :], aexp[:, :],
                                  start=True, stop=True,
                                  skip_group_check=True).then_inc(s_te, 1)
                        for g in range(G):
                            te.wait_ge(s_dv, 120 + 2 * g)
                            for k in range(NKC):
                                mm = te.matmul(
                                    psum[:, 3072:3200],
                                    yT[:, k * 2048 + g * 128:
                                       k * 2048 + g * 128 + 128],
                                    astk[:, k * 128:(k + 1) * 128],
                                    start=(k == 0), stop=(k == NKC - 1),
                                    skip_group_check=True)
                            mm.then_inc(s_te, 1)
                            te.wait_ge(s_dv, 120 + 2 * g + 1)
                            bc7 = psum[:, 3584:4096].bitcast(BF16)
                            te.matmul(bc7[:, 0:128], uwf[:, :], idb[:, :],
                                      is_transpose=True, start=True, stop=True,
                                      skip_group_check=True).then_inc(s_te, 1)

                    @blk.vector
                    def _(v):
                        hv = [0]

                        def hop(instr):
                            instr.then_inc(s_hv2, 1)
                            hv[0] += 1
                            v.wait_ge(s_hv2, hv[0])

                        v.tensor_copy(rowsc[:, 0:32], gb1r[:, :])
                        v.tensor_copy(rowsc[:, 32:64], lngr[:, :])
                        v.tensor_copy(rowsc[:, 64:96], lnbr[:, :])
                        v.tensor_copy(rowsc[:, 96:104],
                                      gb2r[:, :]).then_inc(s_dv, 1)
                        v.wait_ge(s_te, 1)
                        v.tensor_copy(reps[:, :],
                                      psum[:, 2560:2664]).then_inc(s_dv, 1)
                        v.wait_ge(s_dv, 2)
                        for g in range(G):
                            v.wait_ge(s_te, 2 + 3 * g)
                            hop(v.tensor_tensor(h_sb[:, :], psum[:, 0:32],
                                                reps[:, 0:32], ALU.add))
                            v.nop().then_inc(s_dv, 1)
                            v.tensor_reduce(gl[:, 0:1], h_sb[:, :], AX.X,
                                            ALU.add).then_inc(s_dv, 1)
                            v.wait_ge(s_ac, 3 * g + 1)
                            v.wait_ge(s_dv, 2 + 7 * g + 2)
                            v.tensor_scalar(gl[:, 2:3], gl[:, 0:1], 1.0 / 32.0,
                                            None, ALU.mult)
                            hop(v.tensor_scalar(gl[:, 3:4], gl[:, 1:2],
                                                1.0 / 32.0, None, ALU.mult))
                            hop(v.tensor_tensor(gl[:, 6:7], gl[:, 2:3],
                                                gl[:, 2:3], ALU.mult))
                            v.tensor_tensor(gl[:, 4:5], gl[:, 3:4], gl[:, 6:7],
                                            ALU.subtract).then_inc(s_dv, 1)
                            v.wait_ge(s_ac, 3 * g + 2)
                            hop(v.tensor_tensor(gl[:, 6:7], gl[:, 2:3],
                                                gl[:, 5:6], ALU.mult))
                            hop(v.tensor_scalar(gl[:, 7:8], gl[:, 6:7], -1.0,
                                                None, ALU.mult))
                            v.nop().then_inc(s_dv, 1)
                            v.wait_ge(s_ac, 3 * g + 3)
                            hop(v.tensor_tensor(h2_sb[:, :], h2_sb[:, :],
                                                reps[:, 32:64], ALU.mult))
                            hop(v.tensor_tensor(h2_sb[:, :], h2_sb[:, :],
                                                reps[:, 64:96], ALU.add))
                            hop(v.tensor_scalar(h2_sb[:, :], h2_sb[:, :], 0.0,
                                                None, ALU.max))
                            v.nop().then_inc(s_dv, 1)
                            v.wait_ge(s_te, 3 + 3 * g)
                            v.tensor_copy(hnt[:, :],
                                          psum[0:32, 512:640]).then_inc(s_dv, 1)
                            v.wait_ge(s_te, 4 + 3 * g)
                            v.tensor_tensor(gates_all[:, g * 8:(g + 1) * 8],
                                            psum[:, 1024:1032], reps[:, 96:104],
                                            ALU.add).then_inc(s_dv, 1)
                        v.wait_ge(s_dv, 114)
                        ga = gates_all
                        for e in range(8):
                            for e2 in range(8):
                                op = ALU.is_ge if e2 < e else ALU.is_gt
                                mm = v.tensor_tensor(
                                    cmpbuf[:, e2 * 16:(e2 + 1) * 16],
                                    ga[:, e2::8], ga[:, e::8], op)
                            hop(mm)
                            hop(v.tensor_reduce(
                                beats[:, e::8],
                                cmpbuf[:, :].rearrange(
                                    "p (e2 t) -> p t e2", t=16),
                                AX.X, ALU.add))
                        v.tensor_scalar(oh0[:, :], beats[:, :], 0.0, None,
                                        ALU.is_equal)
                        hop(v.tensor_scalar(oh1[:, :], beats[:, :], 1.0, None,
                                            ALU.is_equal))
                        hop(v.tensor_tensor(tprod[:, :], oh0[:, :], ga[:, :],
                                            ALU.mult))
                        hop(v.tensor_reduce(vv[:, 0:16],
                                            tprod[:, :].rearrange(
                                                "p (g e) -> p g e", e=8),
                                            AX.X, ALU.add))
                        hop(v.tensor_tensor(tprod[:, :], oh1[:, :], ga[:, :],
                                            ALU.mult))
                        hop(v.tensor_reduce(vv[:, 16:32],
                                            tprod[:, :].rearrange(
                                                "p (g e) -> p g e", e=8),
                                            AX.X, ALU.add))
                        v.tensor_tensor(vv[:, 32:48], vv[:, 16:32], vv[:, 0:16],
                                        ALU.subtract).then_inc(s_dv, 1)
                        v.wait_ge(s_ac, 49)
                        hop(v.tensor_scalar(vv[:, 96:112], vv[:, 80:96], 1.0,
                                            None, ALU.add))
                        hop(v.reciprocal(vv[:, 48:64], vv[:, 96:112]))
                        v.tensor_scalar(vv[:, 64:80], vv[:, 48:64], -1.0, 1.0,
                                        ALU.mult, ALU.add).then_inc(s_dv, 1)
                        v.wait_ge(s_te, 51)
                        v.wait_ge(s_ms, 1)
                        v.tensor_copy(cnt_sb[:, 126:127], psum[:, 1536:1537])
                        v.tensor_copy(cnt_sb[:, 127:128],
                                      psum[:, 2048:2049]).then_inc(s_dv, 1)
                        v.wait_ge(s_te, 52)
                        v.tensor_copy(cnt_sb[0:1, 0:16],
                                      psum[0:1, 2560:2576]).then_inc(s_dv, 1)
                        v.wait_ge(s_c2, 16)
                        hop(v.tensor_scalar(alw[:, :], cnt2[:, :], CAP, None,
                                            ALU.is_le))
                        for g in range(G):
                            v.tensor_copy(aexp[:, g * 8:(g + 1) * 8],
                                          alw[:, 0:8])
                            mm = v.tensor_copy(
                                aexp[:, 128 + g * 8:128 + (g + 1) * 8],
                                alw[:, 8:16])
                        hop(mm)
                        v.nop().then_inc(s_dv, 1)
                        v.wait_ge(s_te, 53)
                        hop(v.tensor_copy(arep[:, :], psum[:, 2560:2816]))
                        for g in range(G):
                            v.tensor_scalar(w01[:, g * 8:(g + 1) * 8],
                                            oh0[:, g * 8:(g + 1) * 8],
                                            vv[:, 48 + g:49 + g], None, ALU.mult)
                            mm = v.tensor_scalar(
                                w01[:, 128 + g * 8:128 + (g + 1) * 8],
                                oh1[:, g * 8:(g + 1) * 8],
                                vv[:, 64 + g:65 + g], None, ALU.mult)
                        hop(mm)
                        hop(v.tensor_tensor(w01[:, 0:128], w01[:, 0:128],
                                            arep[:, 0:128], ALU.mult))
                        hop(v.tensor_tensor(w01[:, 128:256], w01[:, 128:256],
                                            arep[:, 128:256], ALU.mult))
                        v.tensor_tensor(w_tok[:, :], w01[:, 0:128],
                                        w01[:, 128:256],
                                        ALU.add).then_inc(s_dv, 1)
                        v.wait_ge(s_dv, 120)
                        for g in range(G):
                            v.wait_ge(s_dv, 120 + 2 * g)
                            for e in range(8):
                                mm = v.tensor_scalar(
                                    wexp[:, e * 16:(e + 1) * 16],
                                    ones16[:, :],
                                    w_tok[:, g * 8 + e:g * 8 + e + 1],
                                    ALPHA, ALU.mult, ALU.mult)
                            hop(mm)
                            v.wait_ge(s_te, 54 + 2 * g)
                            v.tensor_tensor(uwf[:, :], psum[:, 3072:3200],
                                            wexp[:, :],
                                            ALU.mult).then_inc(s_dv, 1)
                            v.wait_ge(s_te, 55 + 2 * g)
                            v.tensor_copy(uwT[:, g * 128:(g + 1) * 128],
                                          bc7q[:, 0:128]).then_inc(s_dv, 1)

            S23 = S1.enter_context(contextlib.ExitStack())
            sums2 = alloc(S23, "sums2", [128, 256], F32)
            sumsq2 = alloc(S23, "sumsq2", [128, 256], F32)
            with contextlib.ExitStack() as S2:
                whalf = [alloc(S2, f"whalf{i}", [128, 8192], BF16) for i in range(2)]
                bslab = [alloc(S2, f"bslab{i}", [128, 256], BF16) for i in range(2)]
                ybig = [alloc(S2, f"ybig{i}", [128, 4096], BF16) for i in range(2)]
                z2big = [alloc(S2, f"z2big{i}", [128, 4096], BF16) for i in range(2)]
                sq_scr = alloc(S2, "sq_scr", [128, 256], BF16)

                with (nc.semaphore("c_wh0") as s_wh0,
                      nc.semaphore("c_wh1") as s_wh1,
                      nc.semaphore("c_bs0") as s_bs0,
                      nc.semaphore("c_bs1") as s_bs1,
                      nc.semaphore("c_yb0") as s_yb0,
                      nc.semaphore("c_yb1") as s_yb1,
                      nc.semaphore("c_mm") as s_mm,
                      nc.semaphore("c_ev") as s_ev,
                      nc.semaphore("c_sq") as s_sq,
                      nc.semaphore("c_zs0") as s_zs0,
                      nc.semaphore("c_zs1") as s_zs1,
                      nc.Block("p2") as blk):

                    @blk.gpsimd
                    def _(gp):
                        for s2 in range(NS2):
                            if s2 >= 2:
                                gp.wait_ge(s_mm, 16 * (s2 - 1))
                            gp.dma_start(
                                whalf[s2 % 2][:, :].rearrange(
                                    "p (k c) -> p k c", c=256),
                                d_wmT[:, s2 * 256:(s2 + 1) * 256].rearrange(
                                    "(k p) c -> p k c", p=128),
                            ).then_inc([s_wh0, s_wh1][s2 % 2], 16)
                            gp.dma_start(
                                bslab[s2 % 2][:, :],
                                d_bstk[:, s2 * 256:(s2 + 1) * 256],
                            ).then_inc([s_bs0, s_bs1][s2 % 2], 16)

                    @blk.sync
                    def _(sy):
                        for s2 in range(NS2):
                            if s2 >= 2:
                                sy.wait_ge(s_ev, 16 * (s2 - 1))
                            sy.dma_start(
                                ybig[s2 % 2][:, :].rearrange(
                                    "p (g c) -> p g c", c=256),
                                d_y[:, s2 * 256:(s2 + 1) * 256].rearrange(
                                    "(g p) c -> p g c", p=128),
                            ).then_inc([s_yb0, s_yb1][s2 % 2], 16)
                            sy.wait_ge(s_ev, 16 * (s2 + 1))
                            sy.wait_ge(s_sq, 16 * (s2 + 1))
                            sy.dma_start(
                                d_z2[:, s2 * 256:(s2 + 1) * 256].rearrange(
                                    "(g p) c -> p g c", p=128),
                                z2big[s2 % 2][:, :].rearrange(
                                    "p (g c) -> p g c", c=256),
                            ).then_inc([s_zs0, s_zs1][s2 % 2], 16)

                    @blk.tensor
                    def _(te):
                        for s2 in range(NS2):
                            te.wait_ge([s_wh0, s_wh1][s2 % 2], 16 * (s2 // 2 + 1))
                            te.wait_ge([s_bs0, s_bs1][s2 % 2], 16 * (s2 // 2 + 1))
                            for g in range(G):
                                idx = s2 * 16 + g
                                if idx >= 8:
                                    te.wait_ge(s_ev, idx - 7)
                                b = (g % 8) * 512
                                for k in range(NKC):
                                    te.matmul(
                                        psum[:, b:b + 256],
                                        yT[:, k * 2048 + g * 128:
                                           k * 2048 + g * 128 + 128],
                                        whalf[s2 % 2][:, k * 256:(k + 1) * 256],
                                        start=(k == 0), stop=False,
                                        skip_group_check=True)
                                mm = te.matmul(
                                    psum[:, b:b + 256],
                                    uwT[:, g * 128:(g + 1) * 128],
                                    bslab[s2 % 2][:, :],
                                    start=False, stop=True,
                                    skip_group_check=True)
                                mm.then_inc(s_mm, 1)

                    @blk.vector
                    def _(v):
                        for s2 in range(NS2):
                            v.wait_ge([s_yb0, s_yb1][s2 % 2], 16 * (s2 // 2 + 1))
                            if s2 >= 2:
                                v.wait_ge([s_zs0, s_zs1][s2 % 2], 16 * (s2 // 2))
                                v.wait_ge(s_sq, 16 * (s2 - 1))
                            for g in range(G):
                                idx = s2 * 16 + g
                                v.wait_ge(s_mm, idx + 1)
                                b = (g % 8) * 512
                                v.scalar_tensor_tensor(
                                    z2big[s2 % 2][:, g * 256:(g + 1) * 256],
                                    psum[:, b:b + 256], 1.0,
                                    ybig[s2 % 2][:, g * 256:(g + 1) * 256],
                                    ALU.mult, ALU.add,
                                    accum_out=sums2[:, g * 16 + s2:
                                                    g * 16 + s2 + 1],
                                ).then_inc(s_ev, 1)

                    @blk.scalar
                    def _(sc):
                        for s2 in range(NS2):
                            for g in range(G):
                                idx = s2 * 16 + g
                                sc.wait_ge(s_ev, idx + 1)
                                if idx >= 1:
                                    sc.wait_ge(s_sq, idx)
                                sc.activation(
                                    sq_scr[:, :],
                                    z2big[s2 % 2][:, g * 256:(g + 1) * 256],
                                    AF.Square, bias=zvec[:, :], scale=1.0,
                                    accum_out=sumsq2[:, g * 16 + s2:
                                                     g * 16 + s2 + 1],
                                ).then_inc(s_sq, 1)
                        sc.wait_ge(s_zs0, 16 * (NS2 // 2))
                        sc.wait_ge(s_zs1, 16 * (NS2 // 2))

            with contextlib.ExitStack() as S3:
                z2g = [alloc(S3, f"z2g{i}", [128, 4096], BF16) for i in range(4)]
                outg = [alloc(S3, f"outg{i}", [128, 4096], F32) for i in range(2)]
                l2 = alloc(S3, "l2", [128, 8], F32)

                SEMS = contextlib.ExitStack()
                s_zl = [SEMS.enter_context(nc.semaphore(f"o_zl{i}"))
                        for i in range(4)]
                s_dbg = SEMS.enter_context(nc.semaphore("o_dbg"))
                s_dv3 = SEMS.enter_context(nc.semaphore("o_dv"))
                s_ac3 = SEMS.enter_context(nc.semaphore("o_ac"))
                s_st3 = [SEMS.enter_context(nc.semaphore(f"o_st{i}"))
                         for i in range(2)]
                s_hv3 = SEMS.enter_context(nc.semaphore("o_hv"))
                s_ha3 = SEMS.enter_context(nc.semaphore("o_ha"))
                with SEMS, nc.Block("p3") as blk:

                    @blk.sync
                    def _(sy):
                        if DEBUG:
                            sy.dma_start(d_dbgqk[:, :], d_qk[:, :]).then_inc(s_dbg, 16)
                            sy.dma_start(d_dbgv[:, :, :], d_vfm[:, :, :]).then_inc(s_dbg, 16)
                            sy.dma_start(d_dbgy[:, :], d_y[:, :]).then_inc(s_dbg, 16)
                            sy.dma_start(d_dbgz[:, :], d_z2[:, :]).then_inc(s_dbg, 16)
                            sy.dma_start(d_dbgwq[:, :, :], d_wqgk[:, :, :]).then_inc(s_dbg, 16)
                            sy.wait_ge(s_dbg, 16 * 5)
                        for g in range(G):
                            if g >= 4:
                                sy.wait_ge(s_ac3, 2 * g - 6)
                            sy.dma_start(z2g[g % 4][:, :],
                                         d_z2[g * 128:(g + 1) * 128, :]
                                         ).then_inc(s_zl[g % 4], 16)
                            sy.wait_ge(s_ac3, 2 * g + 2)
                            sy.dma_start(d_out[g * 128:(g + 1) * 128, :],
                                         outg[g % 2][:, :]).then_inc(s_st3[g % 2], 16)
                        sy.wait_ge(s_st3[0], 16 * 8)
                        sy.wait_ge(s_st3[1], 16 * 8)

                    @blk.vector
                    def _(v):
                        hv = [0]

                        def hop(instr):
                            instr.then_inc(s_hv3, 1)
                            hv[0] += 1
                            v.wait_ge(s_hv3, hv[0])

                        for g in range(G):
                            v.tensor_reduce(l2[:, 0:1],
                                            sums2[:, g * 16:(g + 1) * 16],
                                            AX.X, ALU.add)
                            hop(v.tensor_reduce(l2[:, 1:2],
                                                sumsq2[:, g * 16:(g + 1) * 16],
                                                AX.X, ALU.add))
                            v.tensor_scalar(l2[:, 2:3], l2[:, 0:1],
                                            1.0 / 4096.0, None, ALU.mult)
                            hop(v.tensor_scalar(l2[:, 3:4], l2[:, 1:2],
                                                1.0 / 4096.0, None, ALU.mult))
                            hop(v.tensor_tensor(l2[:, 6:7], l2[:, 2:3],
                                                l2[:, 2:3], ALU.mult))
                            v.tensor_tensor(l2[:, 4:5], l2[:, 3:4], l2[:, 6:7],
                                            ALU.subtract).then_inc(s_dv3, 1)
                            v.wait_ge(s_ac3, 2 * g + 1)
                            hop(v.tensor_tensor(l2[:, 6:7], l2[:, 2:3],
                                                l2[:, 5:6], ALU.mult))
                            hop(v.tensor_scalar(l2[:, 7:8], l2[:, 6:7], -1.0,
                                                None, ALU.mult))
                            v.nop().then_inc(s_dv3, 1)

                    @blk.scalar
                    def _(sc):
                        for g in range(G):
                            sc.wait_ge(s_dv3, 2 * g + 1)
                            sc.activation(l2[:, 4:5], l2[:, 4:5], AF.Ln,
                                          bias=evec[:, :],
                                          scale=1.0).then_inc(s_ha3, 1)
                            sc.wait_ge(s_ha3, g + 1)
                            sc.activation(l2[:, 5:6], l2[:, 4:5], AF.Exp,
                                          bias=zvec[:, :], scale=-0.5).then_inc(s_ac3, 1)
                            sc.wait_ge(s_dv3, 2 * g + 2)
                            sc.wait_ge(s_zl[g % 4], 16 * (g // 4 + 1))
                            if g >= 2:
                                sc.wait_ge(s_st3[g % 2], 16 * (g // 2))
                            sc.activation(outg[g % 2][:, :], z2g[g % 4][:, :],
                                          AF.Identity, bias=l2[:, 7:8],
                                          scale=l2[:, 5:6]).then_inc(s_ac3, 1)

    return nc


_NC_CACHE = [None]


def _bqk4_layout(bq, bk):
    out = np.zeros((36, 1024), dtype=np.float32)
    out[0:4] = bq.reshape(128, 8, 4).transpose(2, 1, 0).reshape(4, 1024)
    out[32] = bk.T.reshape(1024)
    return out


def kernel(**inputs):
    xfull = np.ascontiguousarray(
        np.asarray(inputs["L0_attn_x"], dtype=np.float32).reshape(-1, E))
    Wq = np.asarray(inputs["L0_attn_Wq"], dtype=np.float32)
    bq = np.asarray(inputs["L0_attn_bq"], dtype=np.float32)
    Wk = np.asarray(inputs["L0_attn_Wk"], dtype=np.float32)
    bk = np.asarray(inputs["L0_attn_bk"], dtype=np.float32)
    Wv = np.asarray(inputs["L0_attn_Wv"], dtype=np.float32)
    bv = np.asarray(inputs["L0_attn_bv"], dtype=np.float32)
    W = np.asarray(inputs["L0_ffn_weight"], dtype=np.float32)
    A = np.asarray(inputs["L0_ffn_lora_A"], dtype=np.float32)
    B = np.asarray(inputs["L0_ffn_lora_B"], dtype=np.float32)
    gw1 = np.asarray(inputs["L0_ffn_gw1"], dtype=np.float32)
    gb1 = np.asarray(inputs["L0_ffn_gb1"], dtype=np.float32)
    lng = np.asarray(inputs["L0_ffn_ln_gamma"], dtype=np.float32)
    lnb = np.asarray(inputs["L0_ffn_ln_beta"], dtype=np.float32)
    gw2 = np.asarray(inputs["L0_ffn_gw2"], dtype=np.float32)
    gb2 = np.asarray(inputs["L0_ffn_gb2"], dtype=np.float32)

    shared = {
        "wqr": np.ascontiguousarray(
            Wq.reshape(E, 128, 8, 4).transpose(0, 3, 2, 1)),
        "wkr": np.ascontiguousarray(Wk.transpose(0, 2, 1)),
        "wvr": np.ascontiguousarray(Wv.transpose(0, 2, 1)),
        "bqk4r": _bqk4_layout(bq, bk),
        "ones4r": np.ones((36, 128), dtype=np.float32),
        "bvr": np.ascontiguousarray(bv),
        "wmT": np.ascontiguousarray(W.T),
        "astk": np.ascontiguousarray(
            A.transpose(0, 2, 1).reshape(128, E).T),
        "bstk": np.ascontiguousarray(B.reshape(128, E)),
        "gw1T": np.ascontiguousarray(gw1.T),
        "gw2T": np.ascontiguousarray(gw2.T),
        "gb1r": np.ascontiguousarray(gb1.reshape(1, 32)),
        "lngr": np.ascontiguousarray(lng.reshape(1, 32)),
        "lnbr": np.ascontiguousarray(lnb.reshape(1, 32)),
        "gb2r": np.ascontiguousarray(gb2.reshape(1, 8)),
        "idb": np.eye(128, dtype=BFnp),
        "idf": np.eye(128, dtype=np.float32),
        "onesr": np.ones((1, 128), dtype=np.float32),
        "onesc": np.ones((128, 1), dtype=np.float32),
        "selc": np.ascontiguousarray(
            (np.arange(128)[:, None] % 8 ==
             np.arange(8)[None, :]).astype(np.float32)),
    }

    if _NC_CACHE[0] is None:
        _NC_CACHE[0] = build_nc()
    nc = _NC_CACHE[0]

    in_maps = []
    for c in range(8):
        xs = xfull[c * TC:(c + 1) * TC]
        m = dict(shared)
        m["x"] = np.ascontiguousarray(xs)
        m["xT"] = np.ascontiguousarray(xs.T)
        in_maps.append(m)

    res = run_bass_kernel_spmd(nc, in_maps, core_ids=list(range(8)))
    LAST_RES[0] = res
    out = np.concatenate([np.asarray(r["out"]) for r in res.results], axis=0)
    return out.reshape(np.asarray(inputs["L0_attn_x"]).shape)


LAST_RES = [None]
